# version 14
# baseline (speedup 1.0000x reference)
import math
import numpy as np
from contextlib import ExitStack

import concourse.bass as bass
import concourse.mybir as mybir
from concourse.bass_utils import run_bass_kernel_spmd

import ml_dtypes
_BF = ml_dtypes.bfloat16

F32 = mybir.dt.float32
BF16 = mybir.dt.bfloat16
AF = mybir.ActivationFunctionType
ALU = mybir.AluOpType
AX = mybir.AxisListType

B, S, D = 4, 4096, 2048
NH, HD = 16, 128
NFC = D // 128
TCH_ALL = S // 128
MY_T = S // 2
TCH_MY = MY_T // 128
QT = 4
E, FI, FS = 8, 1408, 5632
MCH = FI // 128
RSCALE = 1.0 / math.sqrt(float(HD))
EPS = 1e-5

PHASES = 99
DBG_SCRATCH = False

TE, SC, VE, GP, SY = "tensor", "scalar", "vector", "gpsimd", "sync"


def _chunk_sets(half):
    return [0, 3, 4, 7] if half == 0 else [1, 2, 5, 6]


PAIRS = [[0, 1], [2, 3], [4, 5], [6, 7]]
OWN0, OWN1 = _chunk_sets(0), _chunk_sets(1)


def _chunk_owner_pos(g):
    gm = g % 8
    if gm in OWN0:
        return 0, (g // 8) * 4 + OWN0.index(gm)
    return 1, (g // 8) * 4 + OWN1.index(gm)


def _my_rows(half):
    A = _chunk_sets(half)
    idx = []
    for t in range(S // 1024):
        for a in A:
            c = 8 * t + a
            idx.extend(range(c * 128, (c + 1) * 128))
    return np.asarray(idx, dtype=np.int64)


def _rope_tables():
    half = HD // 2
    inv = (1.0 / (10000.0 ** (np.arange(half, dtype=np.float32) / np.float32(half)))).astype(np.float32)
    t = np.arange(S, dtype=np.float32)
    freqs = t[:, None] * inv[None, :]
    cos = np.repeat(np.cos(freqs), 2, axis=-1)
    sin = np.repeat(np.sin(freqs), 2, axis=-1)
    return (np.ascontiguousarray(cos.T.astype(np.float32)),
            np.ascontiguousarray(sin.T.astype(np.float32)))


def _perm_T():
    P = np.zeros((HD, HD), dtype=np.float32)
    for i in range(HD // 2):
        P[2 * i, 2 * i + 1] = -1.0
        P[2 * i + 1, 2 * i] = 1.0
    return np.ascontiguousarray(P.T)


def _cmask(half):
    A = _chunk_sets(half)
    m = np.zeros((8, 128, 512), dtype=np.float32)
    tri = (np.arange(128)[:, None] <= np.arange(128)[None, :]).astype(np.float32)
    for r in range(8):
        for sblk, a in enumerate(A):
            if r < a:
                m[r][:, sblk * 128:(sblk + 1) * 128] = 1.0
            elif r == a:
                m[r][:, sblk * 128:(sblk + 1) * 128] = tri
    return m


class KB:
    def __init__(self, nc, es):
        self.nc = nc
        self.es = es
        self.h = {}
        self.n = {}
        self.waited = {}

    def sem(self, *names):
        for nm in names:
            self.h[nm] = self.es.enter_context(self.nc.semaphore(nm))
            self.n[nm] = 0

    def inc(self, inst, nm, k=1):
        inst.then_inc(self.h[nm], k)
        self.n[nm] += k
        return self.n[nm]

    def wait(self, eng, eng_name, nm, v):
        if v <= 0:
            return
        key = (eng_name, nm)
        if self.waited.get(key, 0) >= v:
            return
        eng.wait_ge(self.h[nm], v)
        self.waited[key] = v

    def snap(self):
        return dict(self.n)

    def wait_bar(self, eng, eng_name, snapd):
        for nm, v in snapd.items():
            self.wait(eng, eng_name, nm, v)


def build_program():
    nc = bass.Bass("TRN2", target_bir_lowering=False, debug=False)
    es = ExitStack()

    def inp(name, shape, dt=F32):
        return nc.dram_tensor(name, list(shape), dt, kind="ExternalInput").ap()

    x_my = inp("x_my", (MY_T, D))
    wqTr = inp("wqTr", (NH, 128, D))
    wkTr = inp("wkTr", (NH, 128, D))
    wvTr = inp("wvTr", (4, 128, NFC * 512))
    woT = inp("woT", (D, D))
    bq = inp("bq", (D,))
    bk = inp("bk", (D,))
    bv_row = inp("bv_row", (1, D))
    bo_row = inp("bo_row", (1, D))
    wr = inp("wr", (D, E))
    wsg = inp("wsg", (D, 1))
    wgr = inp("wgr", (E, MCH, 128, D))
    wur = inp("wur", (E, MCH, 128, D))
    wd = inp("wd", (E, FI, D))
    sgr = inp("sgr", (4 * MCH, 128, D))
    sur = inp("sur", (4 * MCH, 128, D))
    sd = inp("sd", (FS, D))
    cosq = inp("cosq", (HD, MY_T), BF16)
    sinq = inp("sinq", (HD, MY_T), BF16)
    permT = inp("permT", (HD, HD), BF16)
    identi = inp("ident", (128, 128), BF16)
    cmask = inp("cmask", (8, 128, 512), BF16)

    out = nc.dram_tensor("out", [MY_T, D], F32, kind="ExternalOutput").ap()

    skind = "ExternalOutput" if DBG_SCRATCH else "Internal"
    s_ko = nc.dram_tensor("s_ko", [NH, HD, MY_T], BF16, kind="Internal").ap()
    s_kg = nc.dram_tensor("s_kg", [2, NH, HD, MY_T], BF16, kind="Internal").ap()
    s_q = nc.dram_tensor("s_q", [NH, HD, MY_T], BF16, kind=skind).ap()
    s_vo = nc.dram_tensor("s_vo", [4, MY_T, 512], BF16, kind="Internal").ap()
    s_vg = nc.dram_tensor("s_vg", [4, 2, MY_T, 512], BF16, kind="Internal").ap()
    nrm_in = nc.dram_tensor("nrm_in", [1, 2], F32, kind="Internal").ap()
    nrm_out = nc.dram_tensor("nrm_out", [1, 2], F32, kind="Internal").ap()
    s_o = nc.dram_tensor("s_o", [NH, HD, MY_T], BF16, kind=skind).ap()
    s_x2 = nc.dram_tensor("s_x2", [MY_T, D], F32, kind=skind).ap()
    s_xn2 = nc.dram_tensor("s_xn2", [NFC, 128, MY_T], BF16, kind=skind).ap()
    s_cr = nc.dram_tensor("s_cr", [E, 2, 1024], BF16, kind=skind).ap()
    sx2v_g = s_x2.rearrange("(c p) d -> c p d", p=128)

    kb = KB(nc, es)
    kb.sem("ld", "st")
    kb.sem("lx0", "lx1", "lw0", "lw1", "lh0", "lh1", "l60", "l61", "l70", "l71")
    kb.sem("lda", "lc0", "lc1")
    kb.sem("sq0", "sq1", "sv0", "sv1", "so0", "so1", "sx0", "sx1", "sn0", "sn1")
    kb.sem("a_sq", "a_rs", "a_var", "a_sd", "a_rc", "a_xn", "a_tr", "a_ev")
    kb.sem("w_cast", "p_mm", "p_ev", "p_rot", "p_cmb", "p_nsq", "p_nmm", "p_nmu")
    kb.sem("b_s", "b_e", "b_m", "b_av", "b_rd", "b_bc", "b_o", "b_rbe")
    kb.sem("c_mm", "c_fo", "c_x2", "c_ct", "r_ex", "r_dv", "r_gt")
    kb.sem("d_g", "d_u", "d_sil", "d_mid", "d_dn", "d_acc", "d_cb", "d_cbm", "d_wd")
    kb.sem("misc", "vch", "lv")
    kb.sem("ccK", "ccV0", "ccV1", "ccV2", "ccV3", "ccN", "n1", "n2", "n3")

    _uid = [0]

    def sbt(owner_es, name, shape, dt):
        _uid[0] += 1
        return owner_es.enter_context(
            nc.sbuf_tensor(f"{name}_{_uid[0]}", list(shape), dt))

    def pst(owner_es, name, shape, dt):
        _uid[0] += 1
        return owner_es.enter_context(
            nc.psum_tensor(f"{name}_{_uid[0]}", list(shape), dt))

    identb = sbt(es, "identb", (128, 128), BF16)
    bob = sbt(es, "bob", (1, D), BF16)
    wrb = sbt(es, "wrb", (128, NFC * E), BF16)
    wsgb = sbt(es, "wsgb", (128, NFC), BF16)
    ones1 = sbt(es, "ones1", (1, 128), BF16)
    onesq = sbt(es, "onesq", (128, 128), BF16)
    onesc = sbt(es, "onesc", (128, 1), BF16)
    ones1f = sbt(es, "ones1f", (1, 128), F32)
    negm = sbt(es, "negm", (128, 1), F32)
    gates = sbt(es, "gates", (128, TCH_MY), F32)
    stat = sbt(es, "stat", (128, 16), F32)
    tiny = sbt(es, "tiny", (1, 8), F32)

    esA = ExitStack()
    runm = sbt(esA, "runm", (1, 2, 512), F32)
    cosqb = sbt(esA, "cosqb", (128, MY_T), BF16)
    sinqb = sbt(esA, "sinqb", (128, MY_T), BF16)
    permb = sbt(esA, "permb", (128, 128), BF16)
    cmaskb = sbt(esA, "cmaskb", (128, 8 * 512), BF16)
    bvb = sbt(esA, "bvb", (1, D), BF16)
    bqf = sbt(esA, "bqf", (128, NH), F32)
    bkf = sbt(esA, "bkf", (128, NH), F32)
    wsgf = sbt(esA, "wsgf", (128, NFC), F32)

    blk = es.enter_context(nc.Block())

    stg_bv = sbt(esA, "stg_bv", (1, D), F32)
    stg_bo = sbt(esA, "stg_bo", (1, D), F32)
    stg_wr = sbt(esA, "stg_wr", (128, NFC * E), F32)
    ld_tok = [0]

    @blk.sync
    def _(sy):
        def one(ap_out, ap_in):
            kb.inc(sy.dma_start(out=ap_out, in_=ap_in), "lda", 16)
        one(identb[:, :], identi)
        one(cosqb[:, :], cosq)
        one(sinqb[:, :], sinq)
        one(permb[:, :], permT)
        for r in range(8):
            one(cmaskb[:, r * 512:(r + 1) * 512], cmask[r])
        one(stg_bv[0:1, :], bv_row)
        one(stg_bo[0:1, :], bo_row)
        wrv = wr.rearrange("(a p) e -> a p e", p=128)
        for a in range(NFC):
            one(stg_wr[:, a * E:(a + 1) * E], wrv[a])
        wsgv = wsg.rearrange("(a p) o -> a p o", p=128)
        for a in range(NFC):
            one(wsgf[:, a:a + 1], wsgv[a])
        for h in range(NH):
            one(bqf[:, h:h + 1], bass.AP(bq.tensor, h * 128, [[1, 128], [1, 1]]))
            one(bkf[:, h:h + 1], bass.AP(bk.tensor, h * 128, [[1, 128], [1, 1]]))
        ld_tok[0] = kb.n["lda"]

    @blk.vector
    def _(v):
        v.memset(ones1[:, :], 1.0)
        v.memset(onesq[:, :], 1.0)
        v.memset(onesc[:, :], 1.0)
        v.memset(ones1f[:, :], 1.0)
        v.memset(runm[:, :, :], 0.0)

    def p2_pre(v):
        kb.wait(v, VE, "lda", ld_tok[0])
        kb.inc(v.tensor_copy(bvb[:, :], stg_bv[0:1, :]), "misc", 1)
        kb.inc(v.tensor_copy(bob[:, :], stg_bo[0:1, :]), "misc", 1)
        kb.inc(v.tensor_copy(wrb[:, :], stg_wr[:, :]), "misc", 1)
        kb.inc(v.tensor_copy(wsgb[:, :], wsgf[:, :]), "misc", 1)

    bar0 = kb.snap()

    def emit_ln_phase(nch, src_fn, dst, tstr, bar, load_eng=None, eng_bars=None):
        bs = kb.snap()
        ebars = eng_bars or {}
        esL = ExitStack()
        xb = sbt(esL, "ln_xb", (128, 2, D), F32)
        sqs = sbt(esL, "ln_sqs", (128, D), BF16)
        xtm = sbt(esL, "ln_xtm", (128, 2, D), BF16)
        ptr = pst(esL, "ln_ptr", [128, 2, 1024], BF16)
        dstv = dst.rearrange("p (i t) -> p i t", t=tstr)

        def sl(p, i):
            return stat[:, p * 8 + i:p * 8 + i + 1]

        def _loader(sy):
            enm = load_eng or SY
            kb.wait_bar(sy, enm, ebars.get(enm, bar))
            for c in range(nch):
                p = c & 1
                kb.wait(sy, enm, "a_sq", bs["a_sq"] + c - 1)
                kb.wait(sy, enm, "a_xn", bs["a_xn"] + c - 1)
                kb.inc(sy.dma_start(out=xb[:, p, :], in_=src_fn(c)), "lx%d" % p, 16)

        if load_eng == GP:
            blk.gpsimd(_loader)
        else:
            blk.sync(_loader)

        @blk.scalar
        def _(s):
            kb.wait_bar(s, SC, ebars.get(SC, bar))
            def evacs(cc):
                pp = cc & 1
                for g in range(4):
                    gi = cc * 4 + g
                    kb.wait(s, SC, "a_tr", bs["a_tr"] + gi + 1)
                    kb.inc(s.copy(dstv[:, 4 * g:4 * g + 4, cc * 128:(cc + 1) * 128],
                                  ptr[:, gi & 1, 0:512].rearrange("p (u w) -> p u w", w=128)),
                           "a_ev", 1)
            for c in range(nch):
                p = c & 1
                kb.wait(s, SC, "lx%d" % p, bs["lx%d" % p] + 16 * (c // 2 + 1))
                kb.wait(s, SC, "a_var", bs["a_var"] + c - 1)
                kb.inc(s.activation(out=sqs[:, :], in_=xb[:, p, :], func=AF.Square,
                                    accum_out=sl(p, 1)), "a_sq", 1)
                kb.wait(s, SC, "a_var", bs["a_var"] + c + 1)
                kb.inc(s.activation(out=sl(p, 5), in_=sl(p, 4), func=AF.Sqrt,
                                    bias=np.float32(EPS), scale=1.0), "a_sd", 1)
                if c > 0:
                    evacs(c - 1)
            evacs(nch - 1)

        @blk.vector
        def _(v):
            kb.wait_bar(v, VE, ebars.get(VE, bar))
            for c in range(nch):
                p = c & 1
                kb.wait(v, VE, "lx%d" % p, bs["lx%d" % p] + 16 * (c // 2 + 1))
                kb.wait(v, VE, "a_var", bs["a_var"] + c - 1)
                kb.inc(v.reduce_sum(out=sl(p, 0), in_=xb[:, p, :], axis=AX.X), "a_rs", 1)
                kb.wait(v, VE, "a_rs", bs["a_rs"] + c + 1)
                kb.wait(v, VE, "vch",
                        kb.inc(v.tensor_scalar(out=sl(p, 2), in0=sl(p, 0),
                                               scalar1=1.0 / D, scalar2=None,
                                               op0=ALU.mult), "vch", 1))
                kb.wait(v, VE, "vch",
                        kb.inc(v.tensor_tensor(out=sl(p, 3), in0=sl(p, 2),
                                               in1=sl(p, 2), op=ALU.mult), "vch", 1))
                kb.wait(v, VE, "a_sq", bs["a_sq"] + c + 1)
                kb.inc(v.scalar_tensor_tensor(out=sl(p, 4), in0=sl(p, 1),
                                              scalar=1.0 / D, in1=sl(p, 3),
                                              op0=ALU.mult, op1=ALU.subtract),
                       "a_var", 1)
                kb.wait(v, VE, "a_sd", bs["a_sd"] + c + 1)
                kb.inc(v.reciprocal(out=sl(p, 6), in_=sl(p, 5)), "a_rc", 1)
                kb.wait(v, VE, "a_rc", bs["a_rc"] + c + 1)
                kb.wait(v, VE, "a_tr", bs["a_tr"] + 4 * (c - 1))
                kb.inc(v.tensor_scalar(out=xtm[:, p, :], in0=xb[:, p, :],
                                       scalar1=sl(p, 2), scalar2=sl(p, 6),
                                       op0=ALU.subtract, op1=ALU.mult), "a_xn", 1)

        @blk.tensor
        def _(t):
            kb.wait_bar(t, TE, ebars.get(TE, bar))
            for c in range(nch):
                p = c & 1
                kb.wait(t, TE, "a_xn", bs["a_xn"] + c + 1)
                for g in range(4):
                    gi = c * 4 + g
                    kb.wait(t, TE, "a_ev", bs["a_ev"] + gi - 1)
                    for u in range(4):
                        i = 4 * g + u
                        tr = t.transpose(ptr[:, gi & 1, u * 128:(u + 1) * 128],
                                         xtm[:, p, i * 128:(i + 1) * 128],
                                         identb[:, :])
                    kb.inc(tr, "a_tr", 1)

        esL.close()
        return kb.snap()

    def emit_qk_phase(xn_sb, xn_tstr, ntile, wsrc, biasf, cosb, sinb, store_fn,
                      bar, g2, pre=None):
        bs = kb.snap()
        esQ = ExitStack()
        wstg = sbt(esQ, "qk_wstg", (128, 2, D // 2), F32)
        wb = sbt(esQ, "qk_wb", (128, 2, D), BF16)
        q0 = sbt(esQ, "qk_q0", (128, 4, 512), BF16)
        rsb = sbt(esQ, "qk_rsb", (128, 4, 512), BF16)
        t1 = sbt(esQ, "qk_t1", (128, 2, 512), F32)
        t2 = sbt(esQ, "qk_t2", (128, 2, 512), F32)
        qr = sbt(esQ, "qk_qr", (128, 2, 512), BF16)
        pq = pst(esQ, "qk_pq", [128, 4, 512], F32)
        pr = pst(esQ, "qk_pr", [128, 2, 512], F32)
        ntot = NH * ntile

        @blk.sync
        def _(sy):
            kb.wait_bar(sy, SY, bar)
            for h in range(NH):
                for k2 in range(2):
                    hk = 2 * h + k2
                    kb.wait(sy, SY, "w_cast", bs["w_cast"] + hk - 1)
                    kb.inc(sy.dma_start(out=wstg[:, hk & 1, :],
                                        in_=wsrc(h)[:, k2 * (D // 2):(k2 + 1) * (D // 2)]),
                           "lw%d" % (hk & 1), 16)

        @blk.vector
        def _(v):
            kb.wait_bar(v, VE, bar)
            if pre is not None:
                pre(v)

            def qr_add(i):
                kb.wait(v, VE, "p_cmb", bs["p_cmb"] + 2 * i + 2)
                kb.wait(v, VE, "sq%d" % (i & 1),
                        bs["sq%d" % (i & 1)] + 16 * (i // 2))
                kb.inc(v.tensor_tensor(out=qr[:, i & 1, :], in0=t1[:, i & 1, :],
                                       in1=t2[:, i & 1, :], op=ALU.add),
                       "p_nsq", 1)

            for h in range(NH):
                ph = h & 1
                kb.wait(v, VE, "p_mm", bs["p_mm"] + ntile * (h - 1))
                for k2 in range(2):
                    hk = 2 * h + k2
                    kb.wait(v, VE, "lw%d" % (hk & 1),
                            bs["lw%d" % (hk & 1)] + 16 * (hk // 2 + 1))
                    kb.inc(v.tensor_copy(wb[:, ph, k2 * (D // 2):(k2 + 1) * (D // 2)],
                                         wstg[:, hk & 1, :]), "w_cast", 1)
                if h >= 1:
                    for j in range(ntile):
                        qr_add((h - 1) * ntile + j)
            for j in range(ntile):
                qr_add((NH - 1) * ntile + j)

        @blk.scalar
        def _(s):
            kb.wait_bar(s, SC, bar)
            for h in range(NH):
                for j in range(ntile):
                    i = h * ntile + j
                    kb.wait(s, SC, "p_mm", bs["p_mm"] + i + 1)
                    kb.wait(s, SC, "p_rot", bs["p_rot"] + i - 3)
                    kb.wait(s, SC, "p_cmb", bs["p_cmb"] + 2 * (i - 3))
                    kb.inc(s.activation(out=q0[:, i % 4, :], in_=pq[:, i % 4, :],
                                        func=AF.Identity, bias=biasf[:, h:h + 1],
                                        scale=1.0), "p_ev", 1)
                    if i >= 2:
                        ii = i - 2
                        kb.wait(s, SC, "p_rot", bs["p_rot"] + ii + 1)
                        kb.wait(s, SC, "p_cmb", bs["p_cmb"] + 2 * (ii - 3))
                        kb.inc(s.copy(rsb[:, ii % 4, :], pr[:, ii & 1, :]),
                               "a_ev", 1)
            for ii in (ntot - 2, ntot - 1):
                kb.wait(s, SC, "p_rot", bs["p_rot"] + ii + 1)
                kb.inc(s.copy(rsb[:, ii % 4, :], pr[:, ii & 1, :]), "a_ev", 1)

        @blk.tensor
        def _(t):
            kb.wait_bar(t, TE, bar)
            for h in range(NH):
                ph = h & 1
                for j in range(ntile):
                    i = h * ntile + j
                    if i >= 2:
                        ii = i - 2
                        kb.wait(t, TE, "p_ev", bs["p_ev"] + ii + 1)
                        kb.wait(t, TE, "a_ev", bs["a_ev"] + ii - 1)
                        kb.inc(t.matmul(pr[:, ii & 1, :], permb[:, :],
                                        q0[:, ii % 4, :], start=True, stop=True),
                               "p_rot", 1)
                    kb.wait(t, TE, "w_cast", bs["w_cast"] + 2 * (h + 1))
                    kb.wait(t, TE, "p_ev", bs["p_ev"] + i - 3)
                    for a in range(NFC):
                        mm = t.matmul(pq[:, i % 4, :], wb[:, ph, a * 128:(a + 1) * 128],
                                      xn_sb[:, a * xn_tstr + j * 512:
                                            a * xn_tstr + j * 512 + 512],
                                      start=(a == 0), stop=(a == NFC - 1))
                    kb.inc(mm, "p_mm", 1)
            for ii in (ntot - 2, ntot - 1):
                kb.wait(t, TE, "p_ev", bs["p_ev"] + ii + 1)
                kb.wait(t, TE, "a_ev", bs["a_ev"] + ii - 1)
                kb.inc(t.matmul(pr[:, ii & 1, :], permb[:, :], q0[:, ii % 4, :],
                                start=True, stop=True), "p_rot", 1)

        @blk.gpsimd
        def _(g):
            kb.wait_bar(g, GP, bar)
            for h in range(NH):
                for j in range(ntile):
                    i = h * ntile + j
                    kb.wait(g, GP, "p_ev", bs["p_ev"] + i + 1)
                    kb.wait(g, GP, "p_nsq", bs["p_nsq"] + i - 1)
                    g.tensor_tensor(out=t1[:, i & 1, :], in0=q0[:, i % 4, :],
                                    in1=cosb[:, j * 512:(j + 1) * 512], op=ALU.mult)
                    kb.wait(g, GP, "a_ev", bs["a_ev"] + i + 1)
                    kb.inc(g.tensor_tensor(out=t2[:, i & 1, :], in0=rsb[:, i % 4, :],
                                           in1=sinb[:, j * 512:(j + 1) * 512],
                                           op=ALU.mult), "p_cmb", 2)
                    if i >= 1:
                        ii = i - 1
                        h2, j2 = ii // ntile, ii % ntile
                        kb.wait(g, GP, "p_nsq", bs["p_nsq"] + ii + 1)
                        kb.inc(g.dma_start(out=store_fn(h2, j2), in_=qr[:, ii & 1, :]),
                               "sq%d" % (ii & 1), 16)
            ii = ntot - 1
            kb.wait(g, GP, "p_nsq", bs["p_nsq"] + ii + 1)
            kb.inc(g.dma_start(out=store_fn(NH - 1, ntile - 1), in_=qr[:, ii & 1, :]),
                   "sq%d" % (ii & 1), 16)

        esQ.close()
        return kb.snap()

    esQscope = ExitStack()
    xn1q = sbt(esQscope, "xn1q", (128, NFC * MY_T), BF16)
    xmyv = x_my.rearrange("(c p) d -> c p d", p=128)
    sqv = s_q.rearrange("h p (j w) -> h j p w", w=512)

    bar1 = bar0
    if PHASES > 1:
        with nc.named_scope("P1_ln1q"):
            bar1 = emit_ln_phase(TCH_MY, lambda c: xmyv[c], xn1q[:, :], MY_T,
                                 bar0, load_eng=GP,
                                 eng_bars={GP: {}, SC: {}, VE: {},
                                           TE: {"lda": 16}})
    bar2 = bar1
    if PHASES > 2:
      with nc.named_scope("P2_q"):
        bar2 = emit_qk_phase(xn1q[:, :], MY_T, QT,
                             lambda h: wqTr[h],
                             bqf, cosqb, sinqb,
                             lambda h, j: sqv[h, j], bar1, 0, pre=p2_pre)
    q_sq = (kb.n["sq0"], kb.n["sq1"])

    skv = s_ko.rearrange("h p (j w) -> h j p w", w=512)

    bar4 = bar2
    if PHASES > 4:
      with nc.named_scope("P4_k"):
        bar4 = emit_qk_phase(xn1q[:, :], MY_T, QT,
                             lambda h: wkTr[h],
                             bkf, cosqb, sinqb,
                             lambda h, j: skv[h, j], bar2, 1)
    kq_sq = (kb.n["sq0"], kb.n["sq1"])

    esN = ExitStack()
    nin = sbt(esN, "n_in", (128, 8, 512), BF16)
    nsq = sbt(esN, "n_sq", (128, 4, 512), BF16)

    if PHASES > 5:
      with nc.named_scope("P4b_v"):
        bs = kb.snap()
        esV = ExitStack()
        vstg = sbt(esV, "v_stg", (128, 2, 2048), F32)
        wvb = sbt(esV, "v_wvb", (128, 4, NFC * 512), BF16)
        vsb = sbt(esV, "v_out", (128, 2, 512), BF16)
        pv = pst(esV, "v_pv", [128, 4, 512], F32)

        @blk.sync
        def _(sy):
            kb.wait_bar(sy, SY, bar4)
            for fh in range(4):
                for q in range(4):
                    ia = fh * 4 + q
                    kb.wait(sy, SY, "w_cast", bs["w_cast"] + ia - 1)
                    kb.inc(sy.dma_start(out=vstg[:, ia & 1, :],
                                        in_=wvTr[fh][:, q * 2048:(q + 1) * 2048]),
                           "lw%d" % (ia & 1), 16)

        @blk.vector
        def _(v):
            kb.wait_bar(v, VE, bar4)

            def cast(ia):
                fh2, q = ia // 4, ia % 4
                kb.wait(v, VE, "lw%d" % (ia & 1),
                        bs["lw%d" % (ia & 1)] + 16 * (ia // 2 + 1))
                kb.inc(v.tensor_copy(wvb[:, fh2, q * 2048:(q + 1) * 2048],
                                     vstg[:, ia & 1, :]), "w_cast", 1)

            for ia in range(4):
                cast(ia)
            for fh in range(4):
                for tc in range(TCH_MY):
                    i = fh * TCH_MY + tc
                    pj = tc & 1
                    if fh == 0 and tc >= 4:
                        cast(tc)
                    kb.wait(v, VE, "p_mm", bs["p_mm"] + i + 1)
                    kb.wait(v, VE, "sv%d" % (i & 1),
                            bs["sv%d" % (i & 1)] + 16 * (i // 2))
                    kb.inc(v.tensor_copy(vsb[:, pj, :], pv[:, i % 4, :]), "p_ev", 1)

        @blk.tensor
        def _(t):
            kb.wait_bar(t, TE, bar4)
            for fh in range(4):
                kb.wait(t, TE, "w_cast", bs["w_cast"] + 4 * (fh + 1))
                for tc in range(TCH_MY):
                    i = fh * TCH_MY + tc
                    kb.wait(t, TE, "p_ev", bs["p_ev"] + i - 3)
                    t.matmul(pv[:, i % 4, :], ones1[:, :],
                             bvb[0:1, fh * 512:(fh + 1) * 512],
                             start=True, stop=False)
                    for a in range(NFC):
                        mm = t.matmul(pv[:, i % 4, :],
                                      xn1q[:, a * MY_T + tc * 128:
                                           a * MY_T + tc * 128 + 128],
                                      wvb[:, fh, a * 512:(a + 1) * 512],
                                      start=False, stop=(a == NFC - 1))
                    kb.inc(mm, "p_mm", 1)

        @blk.gpsimd
        def _(g):
            kb.wait_bar(g, GP, bar4)
            kb.wait(g, GP, "sq0", kq_sq[0])
            kb.wait(g, GP, "sq1", kq_sq[1])
            kb.inc(g.collective_compute(
                "AllGather", ALU.bypass, replica_groups=PAIRS,
                ins=[s_ko.rearrange("h p t -> (h p) t")],
                outs=[s_kg.rearrange("r h p t -> (r h p) t")]), "ccK", 1)
            for fh in range(4):
                for tc in range(TCH_MY):
                    i = fh * TCH_MY + tc
                    pj = tc & 1
                    kb.wait(g, GP, "p_ev", bs["p_ev"] + i + 1)
                    kb.inc(g.dma_start(out=s_vo[fh][tc * 128:(tc + 1) * 128, :],
                                       in_=vsb[:, pj, :]),
                           "sv%d" % (i & 1), 16)
                kb.wait(g, GP, "sv0", bs["sv0"] + 16 * 8 * (fh + 1))
                kb.wait(g, GP, "sv1", bs["sv1"] + 16 * 8 * (fh + 1))
                kb.inc(g.collective_compute(
                    "AllGather", ALU.bypass, replica_groups=PAIRS,
                    ins=[s_vo[fh]],
                    outs=[s_vg[fh].rearrange("r t c -> (r t) c")]),
                       "ccV%d" % fh, 1)

        esV.close()
    bar4c = kb.snap()

    if PHASES > 5:
      with nc.named_scope("P4c_norms"):
        bs = kb.snap()
        pn = pst(esN, "n_pn", [128, 4, 512], F32)
        sqv4 = s_q.rearrange("h p (j w) -> h j p w", w=512)
        skv4 = s_ko.rearrange("h p (j w) -> h j p w", w=512)
        NQ = NH * QT
        NTOT = 2 * NQ

        def tsrc(i):
            if i < NQ:
                return sqv4[i // QT, i % QT]
            k = i - NQ
            return skv4[k // QT, k % QT]

        nsem = ["lx0", "lx1", "lw0", "lw1"]

        @blk.sync
        def _(sy):
            kb.wait(sy, SY, "sq0", q_sq[0])
            kb.wait(sy, SY, "sq1", q_sq[1])
            for i in range(NTOT):
                if i == NQ:
                    kb.wait(sy, SY, "sq0", kq_sq[0])
                    kb.wait(sy, SY, "sq1", kq_sq[1])
                kb.wait(sy, SY, "p_nsq", bs["p_nsq"] + i - 7)
                kb.inc(sy.dma_start(out=nin[:, i % 8, :], in_=tsrc(i)),
                       nsem[i % 4], 16)

        @blk.scalar
        def _(s):
            kb.wait_bar(s, SC, bar4c)
            for i in range(NTOT):
                kb.wait(s, SC, nsem[i % 4],
                        bs[nsem[i % 4]] + 16 * (i // 4 + 1))
                kb.wait(s, SC, "p_nmm", bs["p_nmm"] + i - 3)
                kb.inc(s.activation(out=nsq[:, i % 4, :], in_=nin[:, i % 8, :],
                                    func=AF.Square), "p_nsq", 1)

        @blk.tensor
        def _(t):
            kb.wait_bar(t, TE, bar4c)
            for i in range(NTOT):
                kb.wait(t, TE, "p_nsq", bs["p_nsq"] + i + 1)
                kb.wait(t, TE, "p_nmu", bs["p_nmu"] + i - 3)
                kb.inc(t.matmul(pn[0:1, i % 4, :], onesc[:, :], nsq[:, i % 4, :],
                                start=True, stop=True), "p_nmm", 1)

        @blk.vector
        def _(v):
            kb.wait_bar(v, VE, bar4c)
            for i in range(NTOT):
                g2 = 0 if i < NQ else 1
                kb.wait(v, VE, "p_nmm", bs["p_nmm"] + i + 1)
                kb.wait(v, VE, "p_nmu", bs["p_nmu"] + i)
                kb.inc(v.tensor_tensor(out=runm[0:1, g2, :], in0=runm[0:1, g2, :],
                                       in1=pn[0:1, i % 4, :], op=ALU.max),
                       "p_nmu", 1)

        esN.close()
    esQscope.close()
    barN = kb.snap()

    if PHASES > 5:
        esM = ExitStack()
        pnm = pst(esM, "pnm", [128, 512], F32)
        nrmb = sbt(esM, "nrmb", (1, 2), F32)

        @blk.vector
        def _(v):
            kb.wait(v, VE, "p_nmu", barN["p_nmu"])
            kb.inc(v.tensor_reduce(out=tiny[0:1, 6:7], in_=runm[0:1, 0, :],
                                   axis=AX.X, op=ALU.max), "misc", 1)
            kb.inc(v.tensor_reduce(out=tiny[0:1, 7:8], in_=runm[0:1, 1, :],
                                   axis=AX.X, op=ALU.max), "misc", 1)

        @blk.gpsimd
        def _(g):
            kb.wait(g, GP, "misc", kb.n["misc"])
            kb.inc(g.dma_start(out=nrm_in, in_=tiny[0:1, 6:8]), "lda", 16)
            kb.wait(g, GP, "lda", kb.n["lda"])
            kb.inc(g.collective_compute(
                "AllReduce", ALU.max, replica_groups=PAIRS,
                ins=[nrm_in], outs=[nrm_out]), "ccN", 1)

        @blk.sync
        def _(sy):
            kb.wait(sy, SY, "ccN", 1)
            kb.inc(sy.dma_start(out=nrmb[0:1, :], in_=nrm_out), "lda", 16)

        @blk.scalar
        def _(s):
            kb.wait(s, SC, "lda", kb.n["lda"])
            s.activation(out=tiny[0:1, 0:1], in_=nrmb[0:1, 0:1], func=AF.Sqrt,
                         bias=0.0, scale=1.0)
            kb.inc(s.activation(out=tiny[0:1, 1:2], in_=nrmb[0:1, 1:2],
                                func=AF.Sqrt, bias=0.0, scale=1.0), "misc", 1)

        @blk.vector
        def _(v):
            kb.wait(v, VE, "misc", kb.n["misc"])
            kb.wait(v, VE, "vch",
                    kb.inc(v.tensor_tensor(out=tiny[0:1, 2:3], in0=tiny[0:1, 0:1],
                                           in1=tiny[0:1, 1:2], op=ALU.mult),
                           "vch", 1))
            kb.inc(v.tensor_scalar(out=tiny[0:1, 3:4], in0=tiny[0:1, 2:3],
                                   scalar1=-RSCALE, scalar2=None, op0=ALU.mult),
                   "misc", 1)

        @blk.tensor
        def _(t):
            kb.wait(t, TE, "misc", kb.n["misc"])
            kb.inc(t.matmul(pnm[:, 0:1], ones1f[:, :], tiny[0:1, 3:4],
                            start=True, stop=True), "misc", 1)

        @blk.vector
        def _(v):
            kb.wait(v, VE, "misc", kb.n["misc"])
            kb.inc(v.tensor_copy(negm[:, :], pnm[:, 0:1]), "misc", 1)

        esM.close()
    bar5 = kb.snap()

    def sidx(h, j, kc):
        return h * 80 + [0, 8, 24, 48][j] + kc

    if PHASES > 6:
      with nc.named_scope("P5_attn"):
        bs = kb.snap()
        esOT = ExitStack()
        ot = sbt(esOT, "c_ot", (128, 2, NH * 512), BF16)
        esB = ExitStack()
        kh = sbt(esB, "b_kh", (128, 4, S), BF16)
        vh = sbt(esB, "b_vh", (128, 2, 4 * S), BF16)
        qh = sbt(esB, "b_qh", (128, 4, MY_T), BF16)
        pt = sbt(esB, "b_pt", (128, 4, 512), BF16)
        rb = sbt(esB, "b_rb", (128, 2, 512), F32)
        osb = sbt(esB, "b_o", (128, 2, 512), BF16)
        psS = pst(esB, "b_psS", [128, 4, 512], F32)
        psO = pst(esB, "b_psO", [128, 2, 512], F32)
        psD = pst(esB, "b_psD", [128, 2, 512], F32)
        sov = s_o.rearrange("h p (j w) -> h j p w", w=512)

        def kcol(g):
            o, l = _chunk_owner_pos(g)
            return o * MY_T + l * 128

        def vslot(g):
            o, l = _chunk_owner_pos(g)
            return o * 16 + l

        def order_list(j):
            nk = 8 * (j + 1)
            if j == 0:
                return list(range(8))
            return [0] + list(range(nk - 8, nk)) + list(range(1, nk - 8))

        def mask_r(j, pos):
            if j == 0:
                return pos if pos < 8 else None
            return pos - 1 if 1 <= pos <= 8 else None

        @blk.sync
        def _(sy):
            kb.wait_bar(sy, SY, bar5)
            kb.wait(sy, SY, "ccK", 1)
            for h in range(NH):
                sem = "lh%d" % (h & 1)
                kb.wait(sy, SY, "b_av", bs["b_av"] + 80 * (h - 3))
                for r in range(2):
                    kb.inc(sy.dma_start(out=kh[:, h % 4, r * MY_T:(r + 1) * MY_T],
                                        in_=s_kg[r, h]), sem, 16)
                kb.inc(sy.dma_start(out=qh[:, h % 4, :], in_=s_q[h]), sem, 16)

        @blk.tensor
        def _(t):
            kb.wait_bar(t, TE, bar5)

            def avd(h, j, pos, nk):
                kc = order_list(j)[pos]
                si = sidx(h, j, pos)
                ti = h * QT + j
                r = mask_r(j, pos)
                if r is not None:
                    kb.wait(t, TE, "b_m", bs["b_m"] + ti * 8 + r + 1)
                else:
                    kb.wait(t, TE, "b_e", bs["b_e"] + si + 1)
                if pos == 0:
                    kb.wait(t, TE, "b_o", bs["b_o"] + ti - 1)
                    kb.wait(t, TE, "b_rd", bs["b_rd"] + ti - 1)
                    kb.wait(t, TE, "lv", bs["lv"] + 16 * 32 * (h // 4 + 1))
                pg = (h // 4) & 1
                hh = h % 4
                vc = vslot(kc) * 512 + hh * 128
                t.matmul(psO[:, ti & 1, :], vh[:, pg, vc:vc + 128],
                         pt[:, si % 4, :], start=(pos == 0), stop=(pos == nk - 1))
                kb.inc(t.matmul(psD[:, ti & 1, :], onesq[:, :], pt[:, si % 4, :],
                                start=(pos == 0), stop=(pos == nk - 1)), "b_av", 1)

            for h in range(NH):
                ph = h & 1
                kb.wait(t, TE, "lh%d" % ph, bs["lh%d" % ph] + 16 * 3 * (h // 2 + 1))
                for j in range(QT):
                    nk = 8 * (j + 1)
                    kcs = order_list(j)
                    for pos in range(nk):
                        si = sidx(h, j, pos)
                        kc2 = kcol(kcs[pos])
                        kb.wait(t, TE, "b_e", bs["b_e"] + si - 3)
                        kb.inc(t.matmul(psS[:, si % 4, :],
                                        kh[:, h % 4, kc2:kc2 + 128],
                                        qh[:, h % 4, j * 512:(j + 1) * 512],
                                        start=True, stop=True), "b_s", 1)
                        if pos >= 1:
                            avd(h, j, pos - 1, nk)
                    avd(h, j, nk - 1, nk)

        @blk.scalar
        def _(s):
            kb.wait_bar(s, SC, bar5)
            for h in range(NH):
                for j in range(QT):
                    nk = 8 * (j + 1)
                    for pos in range(nk):
                        si = sidx(h, j, pos)
                        kb.wait(s, SC, "b_s", bs["b_s"] + si + 1)
                        kb.wait(s, SC, "b_av", bs["b_av"] + si - 3)
                        kb.inc(s.activation(
                            out=pt[:, si % 4, :],
                            in_=psS[:, si % 4, :],
                            func=AF.Exp, bias=negm[:, 0:1],
                            scale=RSCALE), "b_e", 1)

        @blk.vector
        def _(v):
            kb.wait_bar(v, VE, bar5)

            def tile_end(ti):
                h2, j2 = ti // QT, ti % QT
                nk2 = 8 * (j2 + 1)
                kb.wait(v, VE, "b_av", bs["b_av"] + sidx(h2, j2, nk2 - 1) + 1)
                kb.inc(v.reciprocal(out=rb[:, ti & 1, :],
                                    in_=psD[:, ti & 1, :]), "b_rd", 1)
                kb.wait(v, VE, "b_rd", bs["b_rd"] + ti + 1)
                kb.wait(v, VE, "so%d" % (ti & 1),
                        bs["so%d" % (ti & 1)] + 16 * (ti // 2))
                kb.inc(v.tensor_tensor(out=osb[:, ti & 1, :],
                                       in0=psO[:, ti & 1, :],
                                       in1=rb[:, ti & 1, :], op=ALU.mult),
                       "b_o", 1)

            for h in range(NH):
                for j in range(QT):
                    ti = h * QT + j
                    pos0 = 0 if j == 0 else 1
                    for pos in range(pos0, pos0 + 8):
                        r = mask_r(j, pos)
                        si = sidx(h, j, pos)
                        kb.wait(v, VE, "b_e", bs["b_e"] + si + 1)
                        kb.inc(v.tensor_tensor(out=pt[:, si % 4, :],
                                               in0=pt[:, si % 4, :],
                                               in1=cmaskb[:, r * 512:(r + 1) * 512],
                                               op=ALU.mult), "b_m", 1)
                    if ti >= 1:
                        tile_end(ti - 1)
            tile_end(NH * QT - 1)

        @blk.gpsimd
        def _(g):
            kb.wait_bar(g, GP, bar5)
            for h in range(NH):
                hg = h // 4
                if h % 4 == 0:
                    load_groups = [0, 1] if h == 0 else ([hg + 1] if hg + 1 < 4 else [])
                    for g2 in load_groups:
                        kb.wait(g, GP, "ccV%d" % g2, 1)
                        if g2 >= 2:
                            kb.wait(g, GP, "b_av", bs["b_av"] + 320 * (g2 - 1))
                        for r in range(2):
                            for ltc in range(16):
                                kb.inc(g.dma_start(
                                    out=vh[:, g2 & 1, (r * 16 + ltc) * 512:
                                           (r * 16 + ltc + 1) * 512],
                                    in_=s_vg[g2, r, ltc * 128:(ltc + 1) * 128, :]),
                                       "lv", 16)
                for j in range(QT):
                    ti = h * QT + j
                    kb.wait(g, GP, "b_o", bs["b_o"] + ti + 1)
                    kb.inc(g.dma_start(out=sov[h, j], in_=osb[:, ti & 1, :]),
                           "so%d" % (ti & 1), 16)

        esB.close()
    bar6 = kb.snap()

    if PHASES > 7:
      with nc.named_scope("P6_wo"):
        bs = kb.snap()
        esC = ExitStack()
        wstg = sbt(esC, "c_wstg", (128, 2, D), F32)
        wob = sbt(esC, "c_wob", (128, NH * D), BF16)
        xm = sbt(esC, "c_xm", (128, 2, D), F32)
        x2 = sbt(esC, "c_x2", (128, 2, D), F32)
        psW = pst(esC, "c_psW", [128, 4, 512], F32)
        wov = woT.rearrange("(h p) d -> h p d", p=128)
        sov2 = s_o.rearrange("h p (c w) -> h c p w", w=512)
        sx2v = s_x2.rearrange("(c p) d -> c p d", p=128)

        @blk.sync
        def _(sy):
            for h in range(NH):
                kb.wait(sy, SY, "so0", 16 * (2 * h + 1))
                kb.inc(sy.dma_start(out=ot[:, 0, h * 512:(h + 1) * 512],
                                    in_=sov2[h, 0]), "lh0", 16)
            kb.wait_bar(sy, SY, bar6)
            for h in range(2):
                kb.inc(sy.dma_start(out=wstg[:, h & 1, :], in_=wov[h]),
                       "lw%d" % (h & 1), 16)
            for h in range(2, NH):
                kb.wait(sy, SY, "w_cast", bs["w_cast"] + h - 1)
                kb.inc(sy.dma_start(out=wstg[:, h & 1, :], in_=wov[h]),
                       "lw%d" % (h & 1), 16)
            for tcg in range(1, TCH_MY // 4):
                pg = tcg & 1
                kb.wait(sy, SY, "c_mm", bs["c_mm"] + 16 * (tcg - 1))
                for h in range(NH):
                    kb.inc(sy.dma_start(out=ot[:, pg, h * 512:(h + 1) * 512],
                                        in_=sov2[h, tcg]), "lh%d" % pg, 16)

        @blk.vector
        def _(v):
            kb.wait_bar(v, VE, bar6)
            for h in range(NH):
                kb.wait(v, VE, "lw%d" % (h & 1),
                        bs["lw%d" % (h & 1)] + 16 * (h // 2 + 1))
                kb.inc(v.tensor_copy(wob[:, h * D:(h + 1) * D], wstg[:, h & 1, :]),
                       "w_cast", 1)
            for tc in range(TCH_MY):
                p = tc & 1
                kb.wait(v, VE, "l6%d" % p, bs["l6%d" % p] + 16 * (tc // 2 + 1))
                kb.wait(v, VE, "sx%d" % p, bs["sx%d" % p] + 16 * (tc // 2))
                for fo in range(4):
                    i = tc * 4 + fo
                    kb.wait(v, VE, "c_mm", bs["c_mm"] + i + 1)
                    kb.inc(v.tensor_tensor(out=x2[:, p, fo * 512:(fo + 1) * 512],
                                           in0=psW[:, i % 4, :],
                                           in1=xm[:, p, fo * 512:(fo + 1) * 512],
                                           op=ALU.add), "c_fo", 1)

        @blk.tensor
        def _(t):
            kb.wait_bar(t, TE, bar6)
            for tc in range(TCH_MY):
                tcg = tc // 4
                pg = tcg & 1
                kb.wait(t, TE, "lh%d" % pg, bs["lh%d" % pg] + 16 * 16 * (tcg // 2 + 1))
                for fo in range(4):
                    i = tc * 4 + fo
                    kb.wait(t, TE, "c_fo", bs["c_fo"] + i - 3)
                    t.matmul(psW[:, i % 4, :], ones1[:, :],
                             bob[0:1, fo * 512:(fo + 1) * 512],
                             start=True, stop=False)
                    for h in range(NH):
                        kb.wait(t, TE, "w_cast", bs["w_cast"] + h + 1)
                        mm = t.matmul(psW[:, i % 4, :],
                                      ot[:, pg, h * 512 + (tc % 4) * 128:
                                         h * 512 + (tc % 4) * 128 + 128],
                                      wob[:, h * D + fo * 512:h * D + (fo + 1) * 512],
                                      start=False, stop=(h == NH - 1))
                    kb.inc(mm, "c_mm", 1)

        @blk.gpsimd
        def _(g):
            kb.wait_bar(g, GP, bar6)
            for tc in range(TCH_MY):
                p = tc & 1
                kb.wait(g, GP, "c_fo", bs["c_fo"] + 4 * (tc - 1))
                kb.inc(g.dma_start(out=xm[:, p, :], in_=xmyv[tc]), "l6%d" % p, 16)
                if tc >= 1:
                    kb.wait(g, GP, "c_fo", bs["c_fo"] + 4 * tc)
                    kb.inc(g.dma_start(out=sx2v[tc - 1], in_=x2[:, (tc - 1) & 1, :]),
                           "sx%d" % ((tc - 1) & 1), 16)
            kb.wait(g, GP, "c_fo", bs["c_fo"] + 4 * TCH_MY)
            kb.inc(g.dma_start(out=sx2v[TCH_MY - 1], in_=x2[:, (TCH_MY - 1) & 1, :]),
                   "sx%d" % ((TCH_MY - 1) & 1), 16)

        esC.close()
        esOT.close()
    esA.close()
    bar6b = kb.snap()

    esC2 = ExitStack()
    xn2f_sb = sbt(esC2, "xn2f", (128, NFC * MY_T), BF16)
    sx2v2 = s_x2.rearrange("(c p) d -> c p d", p=128)
    if PHASES > 7:
      with nc.named_scope("P6b_ln2"):
        bar6c = emit_ln_phase(TCH_MY, lambda c: sx2v2[c], xn2f_sb[:, :], MY_T, bar6b)

        @blk.gpsimd
        def _(g):
            kb.wait_bar(g, GP, bar6c)
            for i in range(NFC):
                kb.inc(g.dma_start(
                    out=s_xn2[i],
                    in_=xn2f_sb[:, i * MY_T:(i + 1) * MY_T]), "sn%d" % (i & 1), 16)
    bar7 = kb.snap()

    if PHASES > 8:
      with nc.named_scope("P7_router"):
        bs = kb.snap()
        esR = ExitStack()
        probs = sbt(esR, "r_pr", (128, E), F32)
        nmax = sbt(esR, "r_nm", (128, 4), F32)
        selm = sbt(esR, "r_sel", (128, E), F32)
        u2 = sbt(esR, "r_u2", (128, E), F32)
        ctm = sbt(esR, "r_ctm", (128, 2, E), BF16)
        crow_st = sbt(esR, "r_crst", (E, MY_T), BF16)
        psR = pst(esR, "r_psR", [128, 2, 512], F32)
        psG = pst(esR, "r_psG", [128, 2, 512], F32)
        psC = pst(esR, "r_psC", [128, 2, 1024], BF16)

        @blk.tensor
        def _(t):
            kb.wait_bar(t, TE, bar7)
            for tc in range(TCH_MY):
                p = tc & 1
                kb.wait(t, TE, "r_ex", bs["r_ex"] + tc - 1)
                for a in range(NFC):
                    mm = t.matmul(psR[:, p, 0:E],
                                  xn2f_sb[:, a * MY_T + tc * 128:
                                          a * MY_T + (tc + 1) * 128],
                                  wrb[:, a * E:(a + 1) * E],
                                  start=(a == 0), stop=(a == NFC - 1))
                kb.wait(t, TE, "r_gt", bs["r_gt"] + tc - 1)
                for a in range(NFC):
                    mm2 = t.matmul(psG[:, p, 0:1],
                                   xn2f_sb[:, a * MY_T + tc * 128:
                                           a * MY_T + (tc + 1) * 128],
                                   wsgb[:, a:a + 1],
                                   start=(a == 0), stop=(a == NFC - 1))
                kb.inc(mm2, "c_mm", 1)
                if tc > 0:
                    kb.wait(t, TE, "r_dv", bs["r_dv"] + 4 * tc)
                    kb.wait(t, TE, "a_ev", bs["a_ev"] + tc - 2)
                    kb.inc(t.transpose(psC[0:E, (tc - 1) & 1, 0:128],
                                       ctm[:, (tc - 1) & 1, :], identb[:, :]),
                           "c_ct", 1)
            kb.wait(t, TE, "r_dv", bs["r_dv"] + 4 * TCH_MY)
            kb.wait(t, TE, "a_ev", bs["a_ev"] + TCH_MY - 2)
            kb.inc(t.transpose(psC[0:E, (TCH_MY - 1) & 1, 0:128],
                               ctm[:, (TCH_MY - 1) & 1, :], identb[:, :]),
                   "c_ct", 1)

        @blk.vector
        def _(v):
            kb.wait_bar(v, VE, bar7)
            for tc in range(TCH_MY):
                p = tc & 1
                kb.wait(v, VE, "c_mm", bs["c_mm"] + tc + 1)
                kb.wait(v, VE, "r_ex", bs["r_ex"] + tc)
                kb.inc(v.tensor_reduce(out=nmax[:, 0:1], in_=psR[:, p, 0:E],
                                       axis=AX.X, op=ALU.max, negate=True),
                       "r_dv", 1)
                kb.wait(v, VE, "r_ex", bs["r_ex"] + tc + 1)
                kb.inc(v.reciprocal(out=nmax[:, 1:2], in_=nmax[:, 1:2]), "r_dv", 1)
                kb.inc(v.tensor_reduce(out=nmax[:, 2:3], in_=probs[:, :],
                                       axis=AX.X, op=ALU.max), "r_dv", 1)
                kb.wait(v, VE, "r_dv", bs["r_dv"] + 4 * tc + 3)
                kb.wait(v, VE, "vch",
                        kb.inc(v.tensor_scalar(out=selm[:, :], in0=probs[:, :],
                                               scalar1=nmax[:, 2:3], scalar2=None,
                                               op0=ALU.is_lt), "vch", 1))
                kb.wait(v, VE, "vch",
                        kb.inc(v.tensor_tensor(out=u2[:, :], in0=probs[:, :],
                                               in1=selm[:, :], op=ALU.mult),
                               "vch", 1))
                kb.inc(v.tensor_reduce(out=nmax[:, 3:4], in_=u2[:, :],
                                       axis=AX.X, op=ALU.max), "misc", 1)
                kb.wait(v, VE, "misc", kb.n["misc"])
                kb.wait(v, VE, "vch",
                        kb.inc(v.tensor_scalar(out=selm[:, :], in0=probs[:, :],
                                               scalar1=nmax[:, 3:4], scalar2=None,
                                               op0=ALU.is_ge), "vch", 1))
                kb.wait(v, VE, "vch",
                        kb.inc(v.scalar_tensor_tensor(out=u2[:, :], in0=probs[:, :],
                                                      scalar=nmax[:, 1:2],
                                                      in1=selm[:, :],
                                                      op0=ALU.mult, op1=ALU.mult),
                               "vch", 1))
                kb.wait(v, VE, "c_ct", bs["c_ct"] + tc - 1)
                kb.inc(v.tensor_copy(ctm[:, p, :], u2[:, :]), "r_dv", 1)
                kb.inc(v.tensor_copy(gates[:, tc:tc + 1], psG[:, p, 0:1]), "r_gt", 1)

        @blk.scalar
        def _(s):
            kb.wait_bar(s, SC, bar7)
            for tc in range(TCH_MY):
                p = tc & 1
                kb.wait(s, SC, "r_dv", bs["r_dv"] + 4 * tc + 1)
                kb.inc(s.activation(out=probs[:, :], in_=psR[:, p, 0:E],
                                    func=AF.Exp, bias=nmax[:, 0:1], scale=1.0,
                                    accum_out=nmax[:, 1:2]), "r_ex", 1)
                if tc > 0:
                    kb.wait(s, SC, "c_ct", bs["c_ct"] + tc)
                    kb.inc(s.copy(crow_st[0:E, (tc - 1) * 128:tc * 128],
                                  psC[0:E, (tc - 1) & 1, 0:128]), "a_ev", 1)
            kb.wait(s, SC, "c_ct", bs["c_ct"] + TCH_MY)
            kb.inc(s.copy(crow_st[0:E, (TCH_MY - 1) * 128:TCH_MY * 128],
                          psC[0:E, (TCH_MY - 1) & 1, 0:128]), "a_ev", 1)
            kb.wait(s, SC, "r_gt", bs["r_gt"] + TCH_MY)
            kb.inc(s.activation(out=gates[:, :], in_=gates[:, :], func=AF.Sigmoid),
                   "misc", 1)

        @blk.gpsimd
        def _(g):
            kb.wait_bar(g, GP, bar7)
            kb.wait(g, GP, "a_ev", bs["a_ev"] + TCH_MY)
            kb.inc(g.dma_start(
                out=s_cr.rearrange("e hh w -> e (hh w)"),
                in_=crow_st[0:E, :]), "st", 16)

        esR.close()
    esC2.close()
    bar8 = kb.snap()

    if PHASES > 9:
      with nc.named_scope("P8_ffn"):
        bs = kb.snap()
        esF = ExitStack()
        acc = sbt(esF, "f_acc", (128, 8 * D), F32)
        xnh = sbt(esF, "f_xnh", (128, NFC * 1024), BF16)
        wstg = sbt(esF, "f_wstg", (128, 2, D), F32)
        wgb = sbt(esF, "f_wgb", (128, 2, D), BF16)
        wub = sbt(esF, "f_wub", (128, 2, D), BF16)
        wdb = sbt(esF, "f_wdb", (128, MCH * D), BF16)
        mid = sbt(esF, "f_mid", (128, MCH * 1024), BF16)
        gsil = sbt(esF, "f_gsil", (128, 2, 512), BF16)
        cbt = sbt(esF, "f_cb", (128, 2, 512), BF16)
        crsl = sbt(esF, "f_crsl", (1, 2, 1024), BF16)
        pg = pst(esF, "f_pg", [128, 2, 512], F32)
        pu = pst(esF, "f_pu", [128, 2, 512], F32)
        pd = pst(esF, "f_pd", [128, 2, 512], F32)
        pcb = pst(esF, "f_pcb", [128, 2, 512], F32)
        wdv = wd.rearrange("e (m p) d -> e m p d", p=128)
        sdv = sd.rearrange("(m p) d -> m p d", p=128)
        outv = out.rearrange("(c p) d -> c p d", p=128)

        jobs = [("e", e) for e in range(E)] + [("s", g2) for g2 in range(4)]
        NJ = len(jobs)
        JW = 3 * MCH
        GU = 2 * MCH

        def wsrcs(kind, idx, m):
            if kind == "e":
                return (wgr[idx, m], wur[idx, m], wdv[idx, m])
            mg = idx * MCH + m
            return (sgr[mg], sur[mg], sdv[mg])

        for half in range(2):
            bsH = kb.snap()
            crsl_tok = {}
            wg_tok = {}
            wu_tok = {}
            wd_tok = {}
            base_tok = [0]

            @blk.sync
            def _(sy):
                if half == 0:
                    kb.wait_bar(sy, SY, bar8)
                else:
                    kb.wait(sy, SY, "d_u", bsH["d_u"])
                for i in range(NFC):
                    kb.inc(sy.dma_start(out=xnh[:, i * 1024:(i + 1) * 1024],
                                        in_=s_xn2[i][:, half * 1024:(half + 1) * 1024]),
                           "lda", 16)
                for j, (kind, idx) in enumerate(jobs):
                    if j == 1:
                        if half == 1:
                            kb.wait(sy, SY, "st", bsH["st"])
                        for tch in range(8):
                            kb.inc(sy.dma_start(out=acc[:, tch * D:(tch + 1) * D],
                                                in_=sx2v_g[half * 8 + tch]),
                                   "lda", 16)
                        base_tok[0] = kb.n["lda"]
                    if kind == "e":
                        kb.wait(sy, SY, "d_cbm", bsH["d_cbm"] + 2 * idx - 2)
                        sem = "lc%d" % (idx & 1)
                        kb.inc(sy.dma_start(out=crsl[0:1, idx & 1, :],
                                            in_=s_cr[idx, half:half + 1]), sem, 16)
                        crsl_tok[idx] = (sem, kb.n[sem])
                    for m in range(MCH):
                        jm = j * MCH + m
                        it = 3 * jm
                        gg, uu, dd = wsrcs(kind, idx, m)
                        kb.wait(sy, SY, "d_wd", bsH["d_wd"] + it - 1)
                        sem = "lw%d" % (it & 1)
                        kb.inc(sy.dma_start(out=wstg[:, it & 1, :], in_=gg),
                               sem, 16)
                        wg_tok[jm] = (sem, kb.n[sem])
                        kb.wait(sy, SY, "d_wd", bsH["d_wd"] + it)
                        sem = "lw%d" % ((it + 1) & 1)
                        kb.inc(sy.dma_start(out=wstg[:, (it + 1) & 1, :], in_=uu),
                               sem, 16)
                        wu_tok[jm] = (sem, kb.n[sem])
                        kb.wait(sy, SY, "d_wd", bsH["d_wd"] + it + 1)
                        sem = "lw%d" % ((it + 2) & 1)
                        kb.inc(sy.dma_start(out=wstg[:, (it + 2) & 1, :], in_=dd),
                               sem, 16)
                        wd_tok[jm] = (sem, kb.n[sem])

            @blk.vector
            def _(v):
                if half == 0:
                    kb.wait_bar(v, VE, bar8)
                for j, (kind, idx) in enumerate(jobs):
                    if kind == "e":
                        for tt in range(2):
                            cbi = 2 * idx + tt
                            kb.wait(v, VE, "d_cbm", bsH["d_cbm"] + cbi + 1)
                            kb.wait(v, VE, "d_mid", bsH["d_mid"] + (j - 1) * GU)
                            kb.inc(v.tensor_copy(cbt[:, tt, :], pcb[:, tt, :]),
                                   "d_cb", 1)
                    def emit_mid(m, tt):
                        gt = j * GU + 2 * m + tt
                        kb.wait(v, VE, "d_u", bsH["d_u"] + gt + 1)
                        kb.wait(v, VE, "d_sil", bsH["d_sil"] + gt + 1)
                        kb.wait(v, VE, "d_dn", bsH["d_dn"] + 32 * (j - 1))
                        mslc = mid[:, m * 1024 + tt * 512:m * 1024 + (tt + 1) * 512]
                        if kind == "e":
                            kb.wait(v, VE, "vch",
                                    kb.inc(v.tensor_tensor(
                                        out=mslc, in0=pu[:, gt & 1, :],
                                        in1=gsil[:, gt & 1, :], op=ALU.mult),
                                        "vch", 1))
                            kb.inc(v.tensor_tensor(
                                out=mslc, in0=mslc, in1=cbt[:, tt, :],
                                op=ALU.mult), "d_mid", 1)
                        else:
                            kb.inc(v.tensor_tensor(
                                out=mslc, in0=pu[:, gt & 1, :],
                                in1=gsil[:, gt & 1, :], op=ALU.mult), "d_mid", 1)

                    for m in range(MCH):
                        jm = j * MCH + m
                        it = 3 * jm
                        gt0 = j * GU + 2 * m
                        kb.wait(v, VE, wg_tok[jm][0], wg_tok[jm][1])
                        kb.wait(v, VE, "d_g", bsH["d_g"] + gt0 - 2)
                        kb.inc(v.tensor_copy(wgb[:, m & 1, :], wstg[:, it & 1, :]),
                               "d_wd", 1)
                        kb.wait(v, VE, wu_tok[jm][0], wu_tok[jm][1])
                        kb.wait(v, VE, "d_u", bsH["d_u"] + gt0 - 2)
                        kb.inc(v.tensor_copy(wub[:, m & 1, :], wstg[:, (it + 1) & 1, :]),
                               "d_wd", 1)
                        kb.wait(v, VE, wd_tok[jm][0], wd_tok[jm][1])
                        kb.wait(v, VE, "d_dn", bsH["d_dn"] + 32 * j)
                        kb.inc(v.tensor_copy(wdb[:, m * D:(m + 1) * D],
                                             wstg[:, (it + 2) & 1, :]), "d_wd", 1)
                        if m >= 1:
                            emit_mid(m - 1, 0)
                            emit_mid(m - 1, 1)
                    emit_mid(MCH - 1, 0)
                    emit_mid(MCH - 1, 1)
                    kb.wait(v, VE, "lda", base_tok[0])
                    for di in range(32):
                        tt, tsub, fo = di // 16, (di // 4) % 4, di % 4
                        tch = tt * 4 + tsub
                        dgi = j * 32 + di
                        kb.wait(v, VE, "d_dn", bsH["d_dn"] + dgi + 1)
                        kb.wait(v, VE, "d_acc", bsH["d_acc"] + (j - 1) * 32 + di + 1)
                        aslc = acc[:, tch * D + fo * 512:tch * D + (fo + 1) * 512]
                        if kind == "e":
                            kb.inc(v.tensor_tensor(out=aslc, in0=aslc,
                                                   in1=pd[:, dgi & 1, :], op=ALU.add),
                                   "d_acc", 1)
                        else:
                            kb.inc(v.scalar_tensor_tensor(
                                out=aslc, in0=pd[:, dgi & 1, :],
                                scalar=gates[:, half * 8 + tch:half * 8 + tch + 1],
                                in1=aslc, op0=ALU.mult, op1=ALU.add), "d_acc", 1)

            @blk.scalar
            def _(s):
                if half == 0:
                    kb.wait_bar(s, SC, bar8)
                for j, (kind, idx) in enumerate(jobs):
                    for m in range(MCH):
                        for tt in range(2):
                            gt = j * GU + 2 * m + tt
                            kb.wait(s, SC, "d_g", bsH["d_g"] + gt + 1)
                            kb.wait(s, SC, "d_mid", bsH["d_mid"] + gt - 1)
                            kb.inc(s.activation(out=gsil[:, gt & 1, :],
                                                in_=pg[:, gt & 1, :], func=AF.Silu),
                                   "d_sil", 1)

            @blk.tensor
            def _(t):
                if half == 0:
                    kb.wait_bar(t, TE, bar8)
                for j, (kind, idx) in enumerate(jobs):
                    if kind == "e":
                        kb.wait(t, TE, crsl_tok[idx][0], crsl_tok[idx][1])
                        for tt in range(2):
                            cbi = 2 * idx + tt
                            kb.wait(t, TE, "d_cb", bsH["d_cb"] + cbi - 1)
                            kb.inc(t.matmul(pcb[:, tt, :], ones1[:, :],
                                            crsl[0:1, idx & 1,
                                                 tt * 512:(tt + 1) * 512],
                                            start=True, stop=True), "d_cbm", 1)
                    for m in range(MCH):
                        kb.wait(t, TE, "d_wd", bsH["d_wd"] + 3 * (j * MCH + m) + 1)
                        for tt in range(2):
                            gt = j * GU + 2 * m + tt
                            kb.wait(t, TE, "d_sil", bsH["d_sil"] + gt - 1)
                            for a in range(NFC):
                                kb.wait(t, TE, "lda", bsH["lda"] + 16 * (a + 1))
                                mm = t.matmul(
                                    pg[:, gt & 1, :],
                                    wgb[:, m & 1, a * 128:(a + 1) * 128],
                                    xnh[:, a * 1024 + tt * 512:a * 1024 + (tt + 1) * 512],
                                    start=(a == 0), stop=(a == NFC - 1))
                            kb.inc(mm, "d_g", 1)
                            kb.wait(t, TE, "d_wd",
                                    bsH["d_wd"] + 3 * (j * MCH + m) + 2)
                            kb.wait(t, TE, "d_mid", bsH["d_mid"] + gt - 1)
                            for a in range(NFC):
                                mm = t.matmul(
                                    pu[:, gt & 1, :],
                                    wub[:, m & 1, a * 128:(a + 1) * 128],
                                    xnh[:, a * 1024 + tt * 512:a * 1024 + (tt + 1) * 512],
                                    start=(a == 0), stop=(a == NFC - 1))
                            kb.inc(mm, "d_u", 1)
                    kb.wait(t, TE, "d_mid", bsH["d_mid"] + (j + 1) * GU)
                    kb.wait(t, TE, "d_wd", bsH["d_wd"] + 3 * (j + 1) * MCH)
                    for di in range(32):
                        tt, tsub, fo = di // 16, (di // 4) % 4, di % 4
                        dgi = j * 32 + di
                        kb.wait(t, TE, "d_acc", bsH["d_acc"] + dgi - 1)
                        for m in range(MCH):
                            mm = t.matmul(
                                pd[:, dgi & 1, :],
                                mid[:, m * 1024 + tt * 512 + tsub * 128:
                                    m * 1024 + tt * 512 + tsub * 128 + 128],
                                wdb[:, m * D + fo * 512:m * D + (fo + 1) * 512],
                                start=(m == 0), stop=(m == MCH - 1))
                        kb.inc(mm, "d_dn", 1)

            @blk.gpsimd
            def _(g):
                if half == 0:
                    kb.wait_bar(g, GP, bar8)
                for tch in range(8):
                    di_last = (tch // 4) * 16 + (tch % 4) * 4 + 3
                    kb.wait(g, GP, "d_acc",
                            bsH["d_acc"] + (NJ - 1) * 32 + di_last + 1)
                    kb.inc(g.dma_start(out=outv[half * 8 + tch],
                                       in_=acc[:, tch * D:(tch + 1) * D]), "st", 16)

        esF.close()

    @blk.gpsimd
    def _(g):
        kb.wait(g, GP, "st", kb.n["st"])

    es.close()
    return nc


_NC_CACHE = {}


def _get_nc():
    key = (PHASES, DBG_SCRATCH)
    if key not in _NC_CACHE:
        _NC_CACHE[key] = build_program()
    return _NC_CACHE[key]


def make_in_maps(inputs):
    f32 = lambda a: np.ascontiguousarray(np.asarray(a, dtype=np.float32))
    x = f32(inputs["L0_attn_x"]).reshape(B, S, D)
    wq = f32(inputs["L0_attn_Wq"])
    wk = f32(inputs["L0_attn_Wk"])
    wv = f32(inputs["L0_attn_Wv"])
    wo = f32(inputs["L0_attn_Wo"])
    wg = f32(inputs["L0_ffn_Wg"])
    wu = f32(inputs["L0_ffn_Wu"])
    sg = f32(inputs["L0_ffn_Sg"])
    su = f32(inputs["L0_ffn_Su"])

    def headcols(w):
        return np.ascontiguousarray(
            w.reshape(NH, 128, NFC, 128).transpose(0, 3, 2, 1).reshape(NH, 128, D))

    def vgroups(w):
        return np.ascontiguousarray(
            w.reshape(4, 512, NFC, 128).transpose(0, 3, 2, 1).reshape(4, 128, NFC * 512))

    def mcols(w):
        return np.ascontiguousarray(
            w.reshape(E, NFC, 128, MCH, 128).transpose(0, 3, 2, 1, 4).reshape(E, MCH, 128, D))

    def scols(w):
        return np.ascontiguousarray(
            w.reshape(NFC, 128, 4 * MCH, 128).transpose(2, 1, 0, 3).reshape(4 * MCH, 128, D))

    base = {
        "wqTr": headcols(wq),
        "wkTr": headcols(wk),
        "wvTr": vgroups(wv),
        "woT": np.ascontiguousarray(wo.T),
        "bq": f32(inputs["L0_attn_bq"]),
        "bk": f32(inputs["L0_attn_bk"]),
        "bv_row": f32(inputs["L0_attn_bv"]).reshape(1, D),
        "bo_row": f32(inputs["L0_attn_bo"]).reshape(1, D),
        "wr": f32(inputs["L0_ffn_Wr"]),
        "wsg": f32(inputs["L0_ffn_Wsg"]),
        "wgr": mcols(wg),
        "wur": mcols(wu),
        "wd": f32(inputs["L0_ffn_Wd"]),
        "sgr": scols(sg),
        "sur": scols(su),
        "sd": f32(inputs["L0_ffn_Sd"]),
        "permT": _perm_T().astype(_BF),
        "ident": np.eye(128, dtype=np.float32).astype(_BF),
    }
    cosf, sinf = _rope_tables()
    rows = {h: _my_rows(h) for h in (0, 1)}
    cmasks = {h: _cmask(h) for h in (0, 1)}

    in_maps = []
    for c in range(8):
        b, half = c // 2, c % 2
        m = dict(base)
        m["x_my"] = np.ascontiguousarray(x[b][rows[half]])
        m["cosq"] = np.ascontiguousarray(cosf[:, rows[half]]).astype(_BF)
        m["sinq"] = np.ascontiguousarray(sinf[:, rows[half]]).astype(_BF)
        m["cmask"] = cmasks[half].astype(_BF)
        in_maps.append(m)
    return in_maps, rows


def kernel(**inputs):
    in_maps, rows = make_in_maps(inputs)
    nc = _get_nc()
    res = run_bass_kernel_spmd(nc, in_maps, core_ids=list(range(8)))
    outp = np.empty((B, S, D), dtype=np.float32)
    for c in range(8):
        b, half = c // 2, c % 2
        outp[b][rows[half]] = np.asarray(res.results[c]["out"], dtype=np.float32)
    return outp


# version 15
# speedup vs baseline: 1.1359x; 1.1359x over previous
import math
import numpy as np
from contextlib import ExitStack

import concourse.bass as bass
import concourse.mybir as mybir
from concourse.bass_utils import run_bass_kernel_spmd

import ml_dtypes
_BF = ml_dtypes.bfloat16

F32 = mybir.dt.float32
BF16 = mybir.dt.bfloat16
AF = mybir.ActivationFunctionType
ALU = mybir.AluOpType
AX = mybir.AxisListType

B, S, D = 4, 4096, 2048
NH, HD = 16, 128
NFC = D // 128
TCH_ALL = S // 128
MY_T = S // 2
TCH_MY = MY_T // 128
QT = 4
E, FI, FS = 8, 1408, 5632
MCH = FI // 128
RSCALE = 1.0 / math.sqrt(float(HD))
EPS = 1e-5

PHASES = 99
DBG_SCRATCH = False

TE, SC, VE, GP, SY = "tensor", "scalar", "vector", "gpsimd", "sync"


def _chunk_sets(half):
    return [0, 3, 4, 7] if half == 0 else [1, 2, 5, 6]


PAIRS = [[0, 1], [2, 3], [4, 5], [6, 7]]
OWN0, OWN1 = _chunk_sets(0), _chunk_sets(1)


def _chunk_owner_pos(g):
    gm = g % 8
    if gm in OWN0:
        return 0, (g // 8) * 4 + OWN0.index(gm)
    return 1, (g // 8) * 4 + OWN1.index(gm)


def _my_rows(half):
    A = _chunk_sets(half)
    idx = []
    for t in range(S // 1024):
        for a in A:
            c = 8 * t + a
            idx.extend(range(c * 128, (c + 1) * 128))
    return np.asarray(idx, dtype=np.int64)


def _rope_tables():
    half = HD // 2
    inv = (1.0 / (10000.0 ** (np.arange(half, dtype=np.float32) / np.float32(half)))).astype(np.float32)
    t = np.arange(S, dtype=np.float32)
    freqs = t[:, None] * inv[None, :]
    cos = np.repeat(np.cos(freqs), 2, axis=-1)
    sin = np.repeat(np.sin(freqs), 2, axis=-1)
    return (np.ascontiguousarray(cos.T.astype(np.float32)),
            np.ascontiguousarray(sin.T.astype(np.float32)))


def _perm_T():
    P = np.zeros((HD, HD), dtype=np.float32)
    for i in range(HD // 2):
        P[2 * i, 2 * i + 1] = -1.0
        P[2 * i + 1, 2 * i] = 1.0
    return np.ascontiguousarray(P.T)


def _cmask(half):
    A = _chunk_sets(half)
    m = np.zeros((8, 128, 512), dtype=np.float32)
    tri = (np.arange(128)[:, None] <= np.arange(128)[None, :]).astype(np.float32)
    for r in range(8):
        for sblk, a in enumerate(A):
            if r < a:
                m[r][:, sblk * 128:(sblk + 1) * 128] = 1.0
            elif r == a:
                m[r][:, sblk * 128:(sblk + 1) * 128] = tri
    return m


class KB:
    def __init__(self, nc, es):
        self.nc = nc
        self.es = es
        self.h = {}
        self.n = {}
        self.waited = {}

    def sem(self, *names):
        for nm in names:
            self.h[nm] = self.es.enter_context(self.nc.semaphore(nm))
            self.n[nm] = 0

    def inc(self, inst, nm, k=1):
        inst.then_inc(self.h[nm], k)
        self.n[nm] += k
        return self.n[nm]

    def wait(self, eng, eng_name, nm, v):
        if v <= 0:
            return
        key = (eng_name, nm)
        if self.waited.get(key, 0) >= v:
            return
        eng.wait_ge(self.h[nm], v)
        self.waited[key] = v

    def snap(self):
        return dict(self.n)

    def wait_bar(self, eng, eng_name, snapd):
        for nm, v in snapd.items():
            self.wait(eng, eng_name, nm, v)


def build_program():
    nc = bass.Bass("TRN2", target_bir_lowering=False, debug=False)
    es = ExitStack()

    def inp(name, shape, dt=F32):
        return nc.dram_tensor(name, list(shape), dt, kind="ExternalInput").ap()

    x_my = inp("x_my", (MY_T, D))
    wqTr = inp("wqTr", (NH, 128, D))
    wkTr = inp("wkTr", (NH, 128, D))
    wvTr = inp("wvTr", (4, 128, NFC * 512))
    woT = inp("woT", (D, D))
    bq = inp("bq", (D,))
    bk = inp("bk", (D,))
    bv_row = inp("bv_row", (1, D))
    bo_row = inp("bo_row", (1, D))
    wr = inp("wr", (D, E))
    wsg = inp("wsg", (D, 1))
    wgr = inp("wgr", (E, MCH, 128, D))
    wur = inp("wur", (E, MCH, 128, D))
    wd = inp("wd", (E, FI, D))
    sgr = inp("sgr", (4 * MCH, 128, D))
    sur = inp("sur", (4 * MCH, 128, D))
    sd = inp("sd", (FS, D))
    cosq = inp("cosq", (HD, MY_T), BF16)
    sinq = inp("sinq", (HD, MY_T), BF16)
    permT = inp("permT", (HD, HD), BF16)
    identi = inp("ident", (128, 128), BF16)
    cmask = inp("cmask", (8, 128, 512), BF16)

    out = nc.dram_tensor("out", [MY_T, D], F32, kind="ExternalOutput").ap()

    skind = "ExternalOutput" if DBG_SCRATCH else "Internal"
    s_ko = nc.dram_tensor("s_ko", [NH, HD, MY_T], BF16, kind="Internal").ap()
    s_kg = nc.dram_tensor("s_kg", [2, NH, HD, MY_T], BF16, kind="Internal").ap()
    s_q = nc.dram_tensor("s_q", [NH, HD, MY_T], BF16, kind=skind).ap()
    s_vo = nc.dram_tensor("s_vo", [4, MY_T, 512], BF16, kind="Internal").ap()
    s_vg = nc.dram_tensor("s_vg", [4, 2, MY_T, 512], BF16, kind="Internal").ap()
    nrm_in = nc.dram_tensor("nrm_in", [1, 2], F32, kind="Internal").ap()
    nrm_out = nc.dram_tensor("nrm_out", [1, 2], F32, kind="Internal").ap()
    s_o = nc.dram_tensor("s_o", [NH, HD, MY_T], BF16, kind=skind).ap()
    s_x2 = nc.dram_tensor("s_x2", [MY_T, D], F32, kind=skind).ap()
    s_xn2 = nc.dram_tensor("s_xn2", [NFC, 128, MY_T], BF16, kind=skind).ap()
    s_cr = nc.dram_tensor("s_cr", [E, 2, 1024], BF16, kind=skind).ap()
    sx2v_g = s_x2.rearrange("(c p) d -> c p d", p=128)

    kb = KB(nc, es)
    kb.sem("ld", "st")
    kb.sem("lx0", "lx1", "lw0", "lw1", "lh0", "lh1", "l60", "l61", "l70", "l71")
    kb.sem("lda", "lc0", "lc1")
    kb.sem("sq0", "sq1", "sv0", "sv1", "so0", "so1", "sx0", "sx1", "sn0", "sn1")
    kb.sem("a_sq", "a_rs", "a_var", "a_sd", "a_rc", "a_xn", "a_tr", "a_ev")
    kb.sem("w_cast", "p_mm", "p_ev", "p_rot", "p_cmb", "p_nsq", "p_nmm", "p_nmu")
    kb.sem("b_s", "b_e", "b_m", "b_av", "b_rd", "b_bc", "b_o", "b_rbe")
    kb.sem("c_mm", "c_fo", "c_x2", "c_ct", "r_ex", "r_dv", "r_gt")
    kb.sem("d_g", "d_u", "d_sil", "d_mid", "d_dn", "d_acc", "d_cb", "d_cbm", "d_wd")
    kb.sem("misc", "vch", "lv")
    kb.sem("ccK", "ccV0", "ccV1", "ccV2", "ccV3", "ccN", "n1", "n2", "n3")

    _uid = [0]

    def sbt(owner_es, name, shape, dt):
        _uid[0] += 1
        return owner_es.enter_context(
            nc.sbuf_tensor(f"{name}_{_uid[0]}", list(shape), dt))

    def pst(owner_es, name, shape, dt):
        _uid[0] += 1
        return owner_es.enter_context(
            nc.psum_tensor(f"{name}_{_uid[0]}", list(shape), dt))

    identb = sbt(es, "identb", (128, 128), BF16)
    bob = sbt(es, "bob", (1, D), BF16)
    wrb = sbt(es, "wrb", (128, NFC * E), BF16)
    wsgb = sbt(es, "wsgb", (128, NFC), BF16)
    ones1 = sbt(es, "ones1", (1, 128), BF16)
    onesq = sbt(es, "onesq", (128, 128), BF16)
    onesc = sbt(es, "onesc", (128, 1), BF16)
    ones1f = sbt(es, "ones1f", (1, 128), F32)
    negm = sbt(es, "negm", (128, 1), F32)
    gates = sbt(es, "gates", (128, TCH_MY), F32)
    stat = sbt(es, "stat", (128, 16), F32)
    tiny = sbt(es, "tiny", (1, 8), F32)

    esA = ExitStack()
    runm = sbt(esA, "runm", (1, 2, 512), F32)
    cosqb = sbt(esA, "cosqb", (128, MY_T), BF16)
    sinqb = sbt(esA, "sinqb", (128, MY_T), BF16)
    permb = sbt(esA, "permb", (128, 128), BF16)
    cmaskb = sbt(esA, "cmaskb", (128, 8 * 512), BF16)
    bvb = sbt(esA, "bvb", (1, D), BF16)
    bqf = sbt(esA, "bqf", (128, NH), F32)
    bkf = sbt(esA, "bkf", (128, NH), F32)
    wsgf = sbt(esA, "wsgf", (128, NFC), F32)

    blk = es.enter_context(nc.Block())

    stg_bv = sbt(esA, "stg_bv", (1, D), F32)
    stg_bo = sbt(esA, "stg_bo", (1, D), F32)
    stg_wr = sbt(esA, "stg_wr", (128, NFC * E), F32)
    ld_tok = [0]

    @blk.sync
    def _(sy):
        def one(ap_out, ap_in):
            kb.inc(sy.dma_start(out=ap_out, in_=ap_in), "lda", 16)
        one(identb[:, :], identi)
        one(cosqb[:, :], cosq)
        one(sinqb[:, :], sinq)
        one(permb[:, :], permT)
        for r in range(8):
            one(cmaskb[:, r * 512:(r + 1) * 512], cmask[r])
        one(stg_bv[0:1, :], bv_row)
        one(stg_bo[0:1, :], bo_row)
        wrv = wr.rearrange("(a p) e -> a p e", p=128)
        for a in range(NFC):
            one(stg_wr[:, a * E:(a + 1) * E], wrv[a])
        wsgv = wsg.rearrange("(a p) o -> a p o", p=128)
        for a in range(NFC):
            one(wsgf[:, a:a + 1], wsgv[a])
        for h in range(NH):
            one(bqf[:, h:h + 1], bass.AP(bq.tensor, h * 128, [[1, 128], [1, 1]]))
            one(bkf[:, h:h + 1], bass.AP(bk.tensor, h * 128, [[1, 128], [1, 1]]))
        ld_tok[0] = kb.n["lda"]

    @blk.vector
    def _(v):
        v.memset(ones1[:, :], 1.0)
        v.memset(onesq[:, :], 1.0)
        v.memset(onesc[:, :], 1.0)
        v.memset(ones1f[:, :], 1.0)
        v.memset(runm[:, :, :], 0.0)

    def p2_pre(v):
        kb.wait(v, VE, "lda", ld_tok[0])
        kb.inc(v.tensor_copy(bvb[:, :], stg_bv[0:1, :]), "misc", 1)
        kb.inc(v.tensor_copy(bob[:, :], stg_bo[0:1, :]), "misc", 1)
        kb.inc(v.tensor_copy(wrb[:, :], stg_wr[:, :]), "misc", 1)
        kb.inc(v.tensor_copy(wsgb[:, :], wsgf[:, :]), "misc", 1)

    bar0 = kb.snap()

    def emit_ln_phase(nch, src_fn, dst, tstr, bar, load_eng=None, eng_bars=None):
        bs = kb.snap()
        ebars = eng_bars or {}
        esL = ExitStack()
        xb = sbt(esL, "ln_xb", (128, 2, D), F32)
        sqs = sbt(esL, "ln_sqs", (128, D), BF16)
        xtm = sbt(esL, "ln_xtm", (128, 2, D), BF16)
        ptr = pst(esL, "ln_ptr", [128, 2, 1024], BF16)
        dstv = dst.rearrange("p (i t) -> p i t", t=tstr)

        def sl(p, i):
            return stat[:, p * 8 + i:p * 8 + i + 1]

        def _loader(sy):
            enm = load_eng or SY
            kb.wait_bar(sy, enm, ebars.get(enm, bar))
            for c in range(nch):
                p = c & 1
                kb.wait(sy, enm, "a_sq", bs["a_sq"] + c - 1)
                kb.wait(sy, enm, "a_xn", bs["a_xn"] + c - 1)
                kb.inc(sy.dma_start(out=xb[:, p, :], in_=src_fn(c)), "lx%d" % p, 16)

        if load_eng == GP:
            blk.gpsimd(_loader)
        else:
            blk.sync(_loader)

        @blk.scalar
        def _(s):
            kb.wait_bar(s, SC, ebars.get(SC, bar))
            def evacs(cc):
                pp = cc & 1
                for g in range(4):
                    gi = cc * 4 + g
                    kb.wait(s, SC, "a_tr", bs["a_tr"] + gi + 1)
                    kb.inc(s.copy(dstv[:, 4 * g:4 * g + 4, cc * 128:(cc + 1) * 128],
                                  ptr[:, gi & 1, 0:512].rearrange("p (u w) -> p u w", w=128)),
                           "a_ev", 1)
            for c in range(nch):
                p = c & 1
                kb.wait(s, SC, "lx%d" % p, bs["lx%d" % p] + 16 * (c // 2 + 1))
                kb.wait(s, SC, "a_var", bs["a_var"] + c - 1)
                kb.inc(s.activation(out=sqs[:, :], in_=xb[:, p, :], func=AF.Square,
                                    accum_out=sl(p, 1)), "a_sq", 1)
                kb.wait(s, SC, "a_var", bs["a_var"] + c + 1)
                kb.inc(s.activation(out=sl(p, 5), in_=sl(p, 4), func=AF.Sqrt,
                                    bias=np.float32(EPS), scale=1.0), "a_sd", 1)
                if c > 0:
                    evacs(c - 1)
            evacs(nch - 1)

        @blk.vector
        def _(v):
            kb.wait_bar(v, VE, ebars.get(VE, bar))
            for c in range(nch):
                p = c & 1
                kb.wait(v, VE, "lx%d" % p, bs["lx%d" % p] + 16 * (c // 2 + 1))
                kb.wait(v, VE, "a_var", bs["a_var"] + c - 1)
                kb.inc(v.reduce_sum(out=sl(p, 0), in_=xb[:, p, :], axis=AX.X), "a_rs", 1)
                kb.wait(v, VE, "a_rs", bs["a_rs"] + c + 1)
                kb.wait(v, VE, "vch",
                        kb.inc(v.tensor_scalar(out=sl(p, 2), in0=sl(p, 0),
                                               scalar1=1.0 / D, scalar2=None,
                                               op0=ALU.mult), "vch", 1))
                kb.wait(v, VE, "vch",
                        kb.inc(v.tensor_tensor(out=sl(p, 3), in0=sl(p, 2),
                                               in1=sl(p, 2), op=ALU.mult), "vch", 1))
                kb.wait(v, VE, "a_sq", bs["a_sq"] + c + 1)
                kb.inc(v.scalar_tensor_tensor(out=sl(p, 4), in0=sl(p, 1),
                                              scalar=1.0 / D, in1=sl(p, 3),
                                              op0=ALU.mult, op1=ALU.subtract),
                       "a_var", 1)
                kb.wait(v, VE, "a_sd", bs["a_sd"] + c + 1)
                kb.inc(v.reciprocal(out=sl(p, 6), in_=sl(p, 5)), "a_rc", 1)
                kb.wait(v, VE, "a_rc", bs["a_rc"] + c + 1)
                kb.wait(v, VE, "a_tr", bs["a_tr"] + 4 * (c - 1))
                kb.inc(v.tensor_scalar(out=xtm[:, p, :], in0=xb[:, p, :],
                                       scalar1=sl(p, 2), scalar2=sl(p, 6),
                                       op0=ALU.subtract, op1=ALU.mult), "a_xn", 1)

        @blk.tensor
        def _(t):
            kb.wait_bar(t, TE, ebars.get(TE, bar))
            for c in range(nch):
                p = c & 1
                kb.wait(t, TE, "a_xn", bs["a_xn"] + c + 1)
                for g in range(4):
                    gi = c * 4 + g
                    kb.wait(t, TE, "a_ev", bs["a_ev"] + gi - 1)
                    for u in range(4):
                        i = 4 * g + u
                        tr = t.transpose(ptr[:, gi & 1, u * 128:(u + 1) * 128],
                                         xtm[:, p, i * 128:(i + 1) * 128],
                                         identb[:, :])
                    kb.inc(tr, "a_tr", 1)

        esL.close()
        return kb.snap()

    def emit_qk_phase(xn_sb, xn_tstr, ntile, wsrc, biasf, cosb, sinb, store_fn,
                      bar, g2, pre=None):
        bs = kb.snap()
        esQ = ExitStack()
        wstg = sbt(esQ, "qk_wstg", (128, 2, D // 2), F32)
        wb = sbt(esQ, "qk_wb", (128, 2, D), BF16)
        q0 = sbt(esQ, "qk_q0", (128, 4, 512), BF16)
        rsb = sbt(esQ, "qk_rsb", (128, 4, 512), BF16)
        t1 = sbt(esQ, "qk_t1", (128, 2, 512), F32)
        t2 = sbt(esQ, "qk_t2", (128, 2, 512), F32)
        qr = sbt(esQ, "qk_qr", (128, 2, 512), BF16)
        pq = pst(esQ, "qk_pq", [128, 4, 512], F32)
        pr = pst(esQ, "qk_pr", [128, 2, 512], F32)
        ntot = NH * ntile

        @blk.sync
        def _(sy):
            kb.wait_bar(sy, SY, bar)
            for h in range(NH):
                for k2 in range(2):
                    hk = 2 * h + k2
                    kb.wait(sy, SY, "w_cast", bs["w_cast"] + hk - 1)
                    kb.inc(sy.dma_start(out=wstg[:, hk & 1, :],
                                        in_=wsrc(h)[:, k2 * (D // 2):(k2 + 1) * (D // 2)]),
                           "lw%d" % (hk & 1), 16)

        @blk.vector
        def _(v):
            kb.wait_bar(v, VE, bar)
            if pre is not None:
                pre(v)

            def qr_add(i):
                kb.wait(v, VE, "p_cmb", bs["p_cmb"] + 2 * i + 2)
                kb.wait(v, VE, "sq%d" % (i & 1),
                        bs["sq%d" % (i & 1)] + 16 * (i // 2))
                kb.inc(v.tensor_tensor(out=qr[:, i & 1, :], in0=t1[:, i & 1, :],
                                       in1=t2[:, i & 1, :], op=ALU.add),
                       "p_nsq", 1)

            for h in range(NH):
                ph = h & 1
                kb.wait(v, VE, "p_mm", bs["p_mm"] + ntile * (h - 1))
                for k2 in range(2):
                    hk = 2 * h + k2
                    kb.wait(v, VE, "lw%d" % (hk & 1),
                            bs["lw%d" % (hk & 1)] + 16 * (hk // 2 + 1))
                    kb.inc(v.tensor_copy(wb[:, ph, k2 * (D // 2):(k2 + 1) * (D // 2)],
                                         wstg[:, hk & 1, :]), "w_cast", 1)
                if h >= 1:
                    for j in range(ntile):
                        qr_add((h - 1) * ntile + j)
            for j in range(ntile):
                qr_add((NH - 1) * ntile + j)

        @blk.scalar
        def _(s):
            kb.wait_bar(s, SC, bar)
            for h in range(NH):
                for j in range(ntile):
                    i = h * ntile + j
                    kb.wait(s, SC, "p_mm", bs["p_mm"] + i + 1)
                    kb.wait(s, SC, "p_rot", bs["p_rot"] + i - 3)
                    kb.wait(s, SC, "p_cmb", bs["p_cmb"] + 2 * (i - 3))
                    kb.inc(s.activation(out=q0[:, i % 4, :], in_=pq[:, i % 4, :],
                                        func=AF.Identity, bias=biasf[:, h:h + 1],
                                        scale=1.0), "p_ev", 1)
                    if i >= 2:
                        ii = i - 2
                        kb.wait(s, SC, "p_rot", bs["p_rot"] + ii + 1)
                        kb.wait(s, SC, "p_cmb", bs["p_cmb"] + 2 * (ii - 3))
                        kb.inc(s.copy(rsb[:, ii % 4, :], pr[:, ii & 1, :]),
                               "a_ev", 1)
            for ii in (ntot - 2, ntot - 1):
                kb.wait(s, SC, "p_rot", bs["p_rot"] + ii + 1)
                kb.inc(s.copy(rsb[:, ii % 4, :], pr[:, ii & 1, :]), "a_ev", 1)

        @blk.tensor
        def _(t):
            kb.wait_bar(t, TE, bar)
            for h in range(NH):
                ph = h & 1
                for j in range(ntile):
                    i = h * ntile + j
                    if i >= 2:
                        ii = i - 2
                        kb.wait(t, TE, "p_ev", bs["p_ev"] + ii + 1)
                        kb.wait(t, TE, "a_ev", bs["a_ev"] + ii - 1)
                        kb.inc(t.matmul(pr[:, ii & 1, :], permb[:, :],
                                        q0[:, ii % 4, :], start=True, stop=True),
                               "p_rot", 1)
                    kb.wait(t, TE, "w_cast", bs["w_cast"] + 2 * (h + 1))
                    kb.wait(t, TE, "p_ev", bs["p_ev"] + i - 3)
                    for a in range(NFC):
                        mm = t.matmul(pq[:, i % 4, :], wb[:, ph, a * 128:(a + 1) * 128],
                                      xn_sb[:, a * xn_tstr + j * 512:
                                            a * xn_tstr + j * 512 + 512],
                                      start=(a == 0), stop=(a == NFC - 1))
                    kb.inc(mm, "p_mm", 1)
            for ii in (ntot - 2, ntot - 1):
                kb.wait(t, TE, "p_ev", bs["p_ev"] + ii + 1)
                kb.wait(t, TE, "a_ev", bs["a_ev"] + ii - 1)
                kb.inc(t.matmul(pr[:, ii & 1, :], permb[:, :], q0[:, ii % 4, :],
                                start=True, stop=True), "p_rot", 1)

        @blk.gpsimd
        def _(g):
            kb.wait_bar(g, GP, bar)
            for h in range(NH):
                for j in range(ntile):
                    i = h * ntile + j
                    kb.wait(g, GP, "p_ev", bs["p_ev"] + i + 1)
                    kb.wait(g, GP, "p_nsq", bs["p_nsq"] + i - 1)
                    g.tensor_tensor(out=t1[:, i & 1, :], in0=q0[:, i % 4, :],
                                    in1=cosb[:, j * 512:(j + 1) * 512], op=ALU.mult)
                    kb.wait(g, GP, "a_ev", bs["a_ev"] + i + 1)
                    kb.inc(g.tensor_tensor(out=t2[:, i & 1, :], in0=rsb[:, i % 4, :],
                                           in1=sinb[:, j * 512:(j + 1) * 512],
                                           op=ALU.mult), "p_cmb", 2)
                    if i >= 1:
                        ii = i - 1
                        h2, j2 = ii // ntile, ii % ntile
                        kb.wait(g, GP, "p_nsq", bs["p_nsq"] + ii + 1)
                        kb.inc(g.dma_start(out=store_fn(h2, j2), in_=qr[:, ii & 1, :]),
                               "sq%d" % (ii & 1), 16)
            ii = ntot - 1
            kb.wait(g, GP, "p_nsq", bs["p_nsq"] + ii + 1)
            kb.inc(g.dma_start(out=store_fn(NH - 1, ntile - 1), in_=qr[:, ii & 1, :]),
                   "sq%d" % (ii & 1), 16)

        esQ.close()
        return kb.snap()

    esQscope = ExitStack()
    xn1q = sbt(esQscope, "xn1q", (128, NFC * MY_T), BF16)
    xmyv = x_my.rearrange("(c p) d -> c p d", p=128)
    sqv = s_q.rearrange("h p (j w) -> h j p w", w=512)

    bar1 = bar0
    if PHASES > 1:
        with nc.named_scope("P1_ln1q"):
            bar1 = emit_ln_phase(TCH_MY, lambda c: xmyv[c], xn1q[:, :], MY_T,
                                 bar0, load_eng=GP,
                                 eng_bars={GP: {}, SC: {}, VE: {},
                                           TE: {"lda": 16}})
    bar2 = bar1
    if PHASES > 2:
      with nc.named_scope("P2_q"):
        bar2 = emit_qk_phase(xn1q[:, :], MY_T, QT,
                             lambda h: wqTr[h],
                             bqf, cosqb, sinqb,
                             lambda h, j: sqv[h, j], bar1, 0, pre=p2_pre)
    q_sq = (kb.n["sq0"], kb.n["sq1"])

    skv = s_ko.rearrange("h p (j w) -> h j p w", w=512)

    bar4 = bar2
    if PHASES > 4:
      with nc.named_scope("P4_k"):
        bar4 = emit_qk_phase(xn1q[:, :], MY_T, QT,
                             lambda h: wkTr[h],
                             bkf, cosqb, sinqb,
                             lambda h, j: skv[h, j], bar2, 1)
    kq_sq = (kb.n["sq0"], kb.n["sq1"])

    esN = ExitStack()
    nin = sbt(esN, "n_in", (128, 8, 512), BF16)
    nsq = sbt(esN, "n_sq", (128, 4, 512), BF16)

    if PHASES > 5:
      with nc.named_scope("P4b_v"):
        bs = kb.snap()
        esV = ExitStack()
        vstg = sbt(esV, "v_stg", (128, 2, 2048), F32)
        wvb = sbt(esV, "v_wvb", (128, 4, NFC * 512), BF16)
        vsb = sbt(esV, "v_out", (128, 2, 512), BF16)
        pv = pst(esV, "v_pv", [128, 4, 512], F32)

        @blk.sync
        def _(sy):
            kb.wait_bar(sy, SY, bar4)
            for fh in range(4):
                for q in range(4):
                    ia = fh * 4 + q
                    kb.wait(sy, SY, "w_cast", bs["w_cast"] + ia - 1)
                    kb.inc(sy.dma_start(out=vstg[:, ia & 1, :],
                                        in_=wvTr[fh][:, q * 2048:(q + 1) * 2048]),
                           "lw%d" % (ia & 1), 16)

        @blk.vector
        def _(v):
            kb.wait_bar(v, VE, bar4)

            def cast(ia):
                fh2, q = ia // 4, ia % 4
                kb.wait(v, VE, "lw%d" % (ia & 1),
                        bs["lw%d" % (ia & 1)] + 16 * (ia // 2 + 1))
                kb.inc(v.tensor_copy(wvb[:, fh2, q * 2048:(q + 1) * 2048],
                                     vstg[:, ia & 1, :]), "w_cast", 1)

            for ia in range(4):
                cast(ia)
            for fh in range(4):
                for tc in range(TCH_MY):
                    i = fh * TCH_MY + tc
                    pj = tc & 1
                    if fh == 0 and tc >= 4:
                        cast(tc)
                    kb.wait(v, VE, "p_mm", bs["p_mm"] + i + 1)
                    kb.wait(v, VE, "sv%d" % (i & 1),
                            bs["sv%d" % (i & 1)] + 16 * (i // 2))
                    kb.inc(v.tensor_copy(vsb[:, pj, :], pv[:, i % 4, :]), "p_ev", 1)

        @blk.tensor
        def _(t):
            kb.wait_bar(t, TE, bar4)
            for fh in range(4):
                kb.wait(t, TE, "w_cast", bs["w_cast"] + 4 * (fh + 1))
                for tc in range(TCH_MY):
                    i = fh * TCH_MY + tc
                    kb.wait(t, TE, "p_ev", bs["p_ev"] + i - 3)
                    t.matmul(pv[:, i % 4, :], ones1[:, :],
                             bvb[0:1, fh * 512:(fh + 1) * 512],
                             start=True, stop=False)
                    for a in range(NFC):
                        mm = t.matmul(pv[:, i % 4, :],
                                      xn1q[:, a * MY_T + tc * 128:
                                           a * MY_T + tc * 128 + 128],
                                      wvb[:, fh, a * 512:(a + 1) * 512],
                                      start=False, stop=(a == NFC - 1))
                    kb.inc(mm, "p_mm", 1)

        @blk.gpsimd
        def _(g):
            kb.wait_bar(g, GP, bar4)
            kb.wait(g, GP, "sq0", kq_sq[0])
            kb.wait(g, GP, "sq1", kq_sq[1])
            kb.inc(g.collective_compute(
                "AllGather", ALU.bypass, replica_groups=PAIRS,
                ins=[s_ko.rearrange("h p t -> (h p) t")],
                outs=[s_kg.rearrange("r h p t -> (r h p) t")]), "ccK", 1)
            for fh in range(4):
                for tc in range(TCH_MY):
                    i = fh * TCH_MY + tc
                    pj = tc & 1
                    kb.wait(g, GP, "p_ev", bs["p_ev"] + i + 1)
                    kb.inc(g.dma_start(out=s_vo[fh][tc * 128:(tc + 1) * 128, :],
                                       in_=vsb[:, pj, :]),
                           "sv%d" % (i & 1), 16)
                kb.wait(g, GP, "sv0", bs["sv0"] + 16 * 8 * (fh + 1))
                kb.wait(g, GP, "sv1", bs["sv1"] + 16 * 8 * (fh + 1))
                kb.inc(g.collective_compute(
                    "AllGather", ALU.bypass, replica_groups=PAIRS,
                    ins=[s_vo[fh]],
                    outs=[s_vg[fh].rearrange("r t c -> (r t) c")]),
                       "ccV%d" % fh, 1)

        esV.close()
    bar4c = kb.snap()

    if PHASES > 5:
      with nc.named_scope("P4c_norms"):
        bs = kb.snap()
        pn = pst(esN, "n_pn", [128, 4, 512], F32)
        sqv4 = s_q.rearrange("h p (j w) -> h j p w", w=512)
        skv4 = s_ko.rearrange("h p (j w) -> h j p w", w=512)
        NQ = NH * QT
        NTOT = 2 * NQ

        def tsrc(i):
            if i < NQ:
                return sqv4[i // QT, i % QT]
            k = i - NQ
            return skv4[k // QT, k % QT]

        nsem = ["lx0", "lx1", "lw0", "lw1"]

        @blk.sync
        def _(sy):
            kb.wait(sy, SY, "sq0", q_sq[0])
            kb.wait(sy, SY, "sq1", q_sq[1])
            for i in range(NTOT):
                if i == NQ:
                    kb.wait(sy, SY, "sq0", kq_sq[0])
                    kb.wait(sy, SY, "sq1", kq_sq[1])
                kb.wait(sy, SY, "p_nsq", bs["p_nsq"] + i - 7)
                kb.inc(sy.dma_start(out=nin[:, i % 8, :], in_=tsrc(i)),
                       nsem[i % 4], 16)

        @blk.scalar
        def _(s):
            kb.wait_bar(s, SC, bar4c)
            for i in range(NTOT):
                kb.wait(s, SC, nsem[i % 4],
                        bs[nsem[i % 4]] + 16 * (i // 4 + 1))
                kb.wait(s, SC, "p_nmm", bs["p_nmm"] + i - 3)
                kb.inc(s.activation(out=nsq[:, i % 4, :], in_=nin[:, i % 8, :],
                                    func=AF.Square), "p_nsq", 1)

        @blk.tensor
        def _(t):
            kb.wait_bar(t, TE, bar4c)
            for i in range(NTOT):
                kb.wait(t, TE, "p_nsq", bs["p_nsq"] + i + 1)
                kb.wait(t, TE, "p_nmu", bs["p_nmu"] + i - 3)
                kb.inc(t.matmul(pn[0:1, i % 4, :], onesc[:, :], nsq[:, i % 4, :],
                                start=True, stop=True), "p_nmm", 1)

        @blk.vector
        def _(v):
            kb.wait_bar(v, VE, bar4c)
            for i in range(NTOT):
                g2 = 0 if i < NQ else 1
                kb.wait(v, VE, "p_nmm", bs["p_nmm"] + i + 1)
                kb.wait(v, VE, "p_nmu", bs["p_nmu"] + i)
                kb.inc(v.tensor_tensor(out=runm[0:1, g2, :], in0=runm[0:1, g2, :],
                                       in1=pn[0:1, i % 4, :], op=ALU.max),
                       "p_nmu", 1)

        esN.close()
    esQscope.close()
    barN = kb.snap()

    if PHASES > 5:
        esM = ExitStack()
        pnm = pst(esM, "pnm", [128, 512], F32)
        nrmb = sbt(esM, "nrmb", (1, 2), F32)

        @blk.vector
        def _(v):
            kb.wait(v, VE, "p_nmu", barN["p_nmu"])
            kb.inc(v.tensor_reduce(out=tiny[0:1, 6:7], in_=runm[0:1, 0, :],
                                   axis=AX.X, op=ALU.max), "misc", 1)
            kb.inc(v.tensor_reduce(out=tiny[0:1, 7:8], in_=runm[0:1, 1, :],
                                   axis=AX.X, op=ALU.max), "misc", 1)

        @blk.gpsimd
        def _(g):
            kb.wait(g, GP, "misc", kb.n["misc"])
            kb.inc(g.dma_start(out=nrm_in, in_=tiny[0:1, 6:8]), "lda", 16)
            kb.wait(g, GP, "lda", kb.n["lda"])
            kb.inc(g.collective_compute(
                "AllReduce", ALU.max, replica_groups=PAIRS,
                ins=[nrm_in], outs=[nrm_out]), "ccN", 1)

        @blk.sync
        def _(sy):
            kb.wait(sy, SY, "ccN", 1)
            kb.inc(sy.dma_start(out=nrmb[0:1, :], in_=nrm_out), "lda", 16)

        @blk.scalar
        def _(s):
            kb.wait(s, SC, "lda", kb.n["lda"])
            s.activation(out=tiny[0:1, 0:1], in_=nrmb[0:1, 0:1], func=AF.Sqrt,
                         bias=0.0, scale=1.0)
            kb.inc(s.activation(out=tiny[0:1, 1:2], in_=nrmb[0:1, 1:2],
                                func=AF.Sqrt, bias=0.0, scale=1.0), "misc", 1)

        @blk.vector
        def _(v):
            kb.wait(v, VE, "misc", kb.n["misc"])
            kb.wait(v, VE, "vch",
                    kb.inc(v.tensor_tensor(out=tiny[0:1, 2:3], in0=tiny[0:1, 0:1],
                                           in1=tiny[0:1, 1:2], op=ALU.mult),
                           "vch", 1))
            kb.inc(v.tensor_scalar(out=tiny[0:1, 3:4], in0=tiny[0:1, 2:3],
                                   scalar1=-RSCALE, scalar2=None, op0=ALU.mult),
                   "misc", 1)

        @blk.tensor
        def _(t):
            kb.wait(t, TE, "misc", kb.n["misc"])
            kb.inc(t.matmul(pnm[:, 0:1], ones1f[:, :], tiny[0:1, 3:4],
                            start=True, stop=True), "misc", 1)

        @blk.vector
        def _(v):
            kb.wait(v, VE, "misc", kb.n["misc"])
            kb.inc(v.tensor_copy(negm[:, :], pnm[:, 0:1]), "misc", 1)

        esM.close()
    bar5 = kb.snap()

    def sidx(h, j, kc):
        return h * 80 + [0, 8, 24, 48][j] + kc

    if PHASES > 6:
      with nc.named_scope("P5_attn"):
        bs = kb.snap()
        esOT = ExitStack()
        ot = sbt(esOT, "c_ot", (128, 2, NH * 512), BF16)
        esB = ExitStack()
        kh = sbt(esB, "b_kh", (128, 4, S), BF16)
        vh = sbt(esB, "b_vh", (128, 2, 4 * S), BF16)
        qh = sbt(esB, "b_qh", (128, 4, MY_T), BF16)
        pt = sbt(esB, "b_pt", (128, 4, 512), BF16)
        rb = sbt(esB, "b_rb", (128, 2, 512), F32)
        osb = sbt(esB, "b_o", (128, 2, 512), BF16)
        psS = pst(esB, "b_psS", [128, 4, 512], F32)
        psO = pst(esB, "b_psO", [128, 2, 512], F32)
        psD = pst(esB, "b_psD", [128, 2, 512], F32)
        sov = s_o.rearrange("h p (j w) -> h j p w", w=512)

        def kcol(g):
            o, l = _chunk_owner_pos(g)
            return o * MY_T + l * 128

        def vslot(g):
            o, l = _chunk_owner_pos(g)
            return o * 16 + l

        def order_list(j):
            nk = 8 * (j + 1)
            if j == 0:
                return list(range(8))
            return [0] + list(range(nk - 8, nk)) + list(range(1, nk - 8))

        def mask_r(j, pos):
            if j == 0:
                return pos if pos < 8 else None
            return pos - 1 if 1 <= pos <= 8 else None

        @blk.sync
        def _(sy):
            kb.wait_bar(sy, SY, bar5)
            kb.wait(sy, SY, "ccK", 1)
            for h in range(NH):
                sem = "lh%d" % (h & 1)
                kb.wait(sy, SY, "b_av", bs["b_av"] + 80 * (h - 3))
                for r in range(2):
                    kb.inc(sy.dma_start(out=kh[:, h % 4, r * MY_T:(r + 1) * MY_T],
                                        in_=s_kg[r, h]), sem, 16)
                kb.inc(sy.dma_start(out=qh[:, h % 4, :], in_=s_q[h]), sem, 16)

        @blk.tensor
        def _(t):
            kb.wait_bar(t, TE, bar5)

            def avd(h, j, pos, nk):
                kc = order_list(j)[pos]
                si = sidx(h, j, pos)
                ti = h * QT + j
                r = mask_r(j, pos)
                if r is not None:
                    kb.wait(t, TE, "b_m", bs["b_m"] + ti * 8 + r + 1)
                else:
                    kb.wait(t, TE, "b_e", bs["b_e"] + si + 1)
                if pos == 0:
                    kb.wait(t, TE, "b_o", bs["b_o"] + ti - 1)
                    kb.wait(t, TE, "b_rd", bs["b_rd"] + ti - 1)
                    kb.wait(t, TE, "lv", bs["lv"] + 16 * 32 * (h // 4 + 1))
                pg = (h // 4) & 1
                hh = h % 4
                vc = vslot(kc) * 512 + hh * 128
                t.matmul(psO[:, ti & 1, :], vh[:, pg, vc:vc + 128],
                         pt[:, si % 4, :], start=(pos == 0), stop=(pos == nk - 1))
                kb.inc(t.matmul(psD[:, ti & 1, :], onesq[:, :], pt[:, si % 4, :],
                                start=(pos == 0), stop=(pos == nk - 1)), "b_av", 1)

            for h in range(NH):
                ph = h & 1
                kb.wait(t, TE, "lh%d" % ph, bs["lh%d" % ph] + 16 * 3 * (h // 2 + 1))
                for j in range(QT):
                    nk = 8 * (j + 1)
                    kcs = order_list(j)
                    for pos in range(nk):
                        si = sidx(h, j, pos)
                        kc2 = kcol(kcs[pos])
                        kb.wait(t, TE, "b_e", bs["b_e"] + si - 3)
                        kb.inc(t.matmul(psS[:, si % 4, :],
                                        kh[:, h % 4, kc2:kc2 + 128],
                                        qh[:, h % 4, j * 512:(j + 1) * 512],
                                        start=True, stop=True), "b_s", 1)
                        if pos >= 1:
                            avd(h, j, pos - 1, nk)
                    avd(h, j, nk - 1, nk)

        @blk.scalar
        def _(s):
            kb.wait_bar(s, SC, bar5)
            for h in range(NH):
                for j in range(QT):
                    nk = 8 * (j + 1)
                    for pos in range(nk):
                        si = sidx(h, j, pos)
                        kb.wait(s, SC, "b_s", bs["b_s"] + si + 1)
                        kb.wait(s, SC, "b_av", bs["b_av"] + si - 3)
                        kb.inc(s.activation(
                            out=pt[:, si % 4, :],
                            in_=psS[:, si % 4, :],
                            func=AF.Exp, bias=negm[:, 0:1],
                            scale=RSCALE), "b_e", 1)

        @blk.vector
        def _(v):
            kb.wait_bar(v, VE, bar5)

            def tile_end(ti):
                h2, j2 = ti // QT, ti % QT
                nk2 = 8 * (j2 + 1)
                kb.wait(v, VE, "b_av", bs["b_av"] + sidx(h2, j2, nk2 - 1) + 1)
                kb.inc(v.reciprocal(out=rb[:, ti & 1, :],
                                    in_=psD[:, ti & 1, :]), "b_rd", 1)
                kb.wait(v, VE, "b_rd", bs["b_rd"] + ti + 1)
                kb.wait(v, VE, "so%d" % (ti & 1),
                        bs["so%d" % (ti & 1)] + 16 * (ti // 2))
                kb.inc(v.tensor_tensor(out=osb[:, ti & 1, :],
                                       in0=psO[:, ti & 1, :],
                                       in1=rb[:, ti & 1, :], op=ALU.mult),
                       "b_o", 1)

            for h in range(NH):
                for j in range(QT):
                    ti = h * QT + j
                    pos0 = 0 if j == 0 else 1
                    for pos in range(pos0, pos0 + 8):
                        r = mask_r(j, pos)
                        si = sidx(h, j, pos)
                        kb.wait(v, VE, "b_e", bs["b_e"] + si + 1)
                        kb.inc(v.tensor_tensor(out=pt[:, si % 4, :],
                                               in0=pt[:, si % 4, :],
                                               in1=cmaskb[:, r * 512:(r + 1) * 512],
                                               op=ALU.mult), "b_m", 1)
                    if ti >= 1:
                        tile_end(ti - 1)
            tile_end(NH * QT - 1)

        @blk.gpsimd
        def _(g):
            kb.wait_bar(g, GP, bar5)
            for h in range(NH):
                hg = h // 4
                if h % 4 == 0:
                    load_groups = [0, 1] if h == 0 else ([hg + 1] if hg + 1 < 4 else [])
                    for g2 in load_groups:
                        kb.wait(g, GP, "ccV%d" % g2, 1)
                        if g2 >= 2:
                            kb.wait(g, GP, "b_av", bs["b_av"] + 320 * (g2 - 1))
                        for r in range(2):
                            for ltc in range(16):
                                kb.inc(g.dma_start(
                                    out=vh[:, g2 & 1, (r * 16 + ltc) * 512:
                                           (r * 16 + ltc + 1) * 512],
                                    in_=s_vg[g2, r, ltc * 128:(ltc + 1) * 128, :]),
                                       "lv", 16)
                for j in range(QT):
                    ti = h * QT + j
                    kb.wait(g, GP, "b_o", bs["b_o"] + ti + 1)
                    kb.inc(g.dma_start(out=sov[h, j], in_=osb[:, ti & 1, :]),
                           "so%d" % (ti & 1), 16)

        esB.close()
    bar6 = kb.snap()

    if PHASES > 7:
      with nc.named_scope("P6_wo"):
        bs = kb.snap()
        esC = ExitStack()
        wstg = sbt(esC, "c_wstg", (128, 2, D), F32)
        wob = sbt(esC, "c_wob", (128, NH * D), BF16)
        xm = sbt(esC, "c_xm", (128, 2, D), F32)
        x2 = sbt(esC, "c_x2", (128, 2, D), F32)
        psW = pst(esC, "c_psW", [128, 4, 512], F32)
        wov = woT.rearrange("(h p) d -> h p d", p=128)
        sov2 = s_o.rearrange("h p (c w) -> h c p w", w=512)
        sx2v = s_x2.rearrange("(c p) d -> c p d", p=128)

        @blk.sync
        def _(sy):
            for h in range(NH):
                kb.wait(sy, SY, "so0", 16 * (2 * h + 1))
                kb.inc(sy.dma_start(out=ot[:, 0, h * 512:(h + 1) * 512],
                                    in_=sov2[h, 0]), "lh0", 16)
            kb.wait_bar(sy, SY, bar6)
            for h in range(2):
                kb.inc(sy.dma_start(out=wstg[:, h & 1, :], in_=wov[h]),
                       "lw%d" % (h & 1), 16)
            for h in range(2, NH):
                kb.wait(sy, SY, "w_cast", bs["w_cast"] + h - 1)
                kb.inc(sy.dma_start(out=wstg[:, h & 1, :], in_=wov[h]),
                       "lw%d" % (h & 1), 16)
            for tcg in range(1, TCH_MY // 4):
                pg = tcg & 1
                kb.wait(sy, SY, "c_mm", bs["c_mm"] + 16 * (tcg - 1))
                for h in range(NH):
                    kb.inc(sy.dma_start(out=ot[:, pg, h * 512:(h + 1) * 512],
                                        in_=sov2[h, tcg]), "lh%d" % pg, 16)

        @blk.vector
        def _(v):
            kb.wait_bar(v, VE, bar6)
            for h in range(NH):
                kb.wait(v, VE, "lw%d" % (h & 1),
                        bs["lw%d" % (h & 1)] + 16 * (h // 2 + 1))
                kb.inc(v.tensor_copy(wob[:, h * D:(h + 1) * D], wstg[:, h & 1, :]),
                       "w_cast", 1)
            for tc in range(TCH_MY):
                p = tc & 1
                kb.wait(v, VE, "l6%d" % p, bs["l6%d" % p] + 16 * (tc // 2 + 1))
                kb.wait(v, VE, "sx%d" % p, bs["sx%d" % p] + 16 * (tc // 2))
                for fo in range(4):
                    i = tc * 4 + fo
                    kb.wait(v, VE, "c_mm", bs["c_mm"] + i + 1)
                    kb.inc(v.tensor_tensor(out=x2[:, p, fo * 512:(fo + 1) * 512],
                                           in0=psW[:, i % 4, :],
                                           in1=xm[:, p, fo * 512:(fo + 1) * 512],
                                           op=ALU.add), "c_fo", 1)

        @blk.tensor
        def _(t):
            kb.wait_bar(t, TE, bar6)
            for tc in range(TCH_MY):
                tcg = tc // 4
                pg = tcg & 1
                kb.wait(t, TE, "lh%d" % pg, bs["lh%d" % pg] + 16 * 16 * (tcg // 2 + 1))
                for fo in range(4):
                    i = tc * 4 + fo
                    kb.wait(t, TE, "c_fo", bs["c_fo"] + i - 3)
                    t.matmul(psW[:, i % 4, :], ones1[:, :],
                             bob[0:1, fo * 512:(fo + 1) * 512],
                             start=True, stop=False)
                    for h in range(NH):
                        kb.wait(t, TE, "w_cast", bs["w_cast"] + h + 1)
                        mm = t.matmul(psW[:, i % 4, :],
                                      ot[:, pg, h * 512 + (tc % 4) * 128:
                                         h * 512 + (tc % 4) * 128 + 128],
                                      wob[:, h * D + fo * 512:h * D + (fo + 1) * 512],
                                      start=False, stop=(h == NH - 1))
                    kb.inc(mm, "c_mm", 1)

        @blk.gpsimd
        def _(g):
            kb.wait_bar(g, GP, bar6)
            for tc in range(TCH_MY):
                p = tc & 1
                kb.wait(g, GP, "c_fo", bs["c_fo"] + 4 * (tc - 1))
                kb.inc(g.dma_start(out=xm[:, p, :], in_=xmyv[tc]), "l6%d" % p, 16)
                if tc >= 1:
                    kb.wait(g, GP, "c_fo", bs["c_fo"] + 4 * tc)
                    kb.inc(g.dma_start(out=sx2v[tc - 1], in_=x2[:, (tc - 1) & 1, :]),
                           "sx%d" % ((tc - 1) & 1), 16)
            kb.wait(g, GP, "c_fo", bs["c_fo"] + 4 * TCH_MY)
            kb.inc(g.dma_start(out=sx2v[TCH_MY - 1], in_=x2[:, (TCH_MY - 1) & 1, :]),
                   "sx%d" % ((TCH_MY - 1) & 1), 16)

        esC.close()
        esOT.close()
    esA.close()
    bar6b = kb.snap()

    esC2 = ExitStack()
    xn2f_sb = sbt(esC2, "xn2f", (128, NFC * MY_T), BF16)
    sx2v2 = s_x2.rearrange("(c p) d -> c p d", p=128)
    if PHASES > 7:
      with nc.named_scope("P6b_ln2"):
        bar6c = emit_ln_phase(TCH_MY, lambda c: sx2v2[c], xn2f_sb[:, :], MY_T, bar6b)

        @blk.gpsimd
        def _(g):
            kb.wait_bar(g, GP, bar6c)
            for i in range(NFC):
                kb.inc(g.dma_start(
                    out=s_xn2[i],
                    in_=xn2f_sb[:, i * MY_T:(i + 1) * MY_T]), "sn%d" % (i & 1), 16)
    bar7 = kb.snap()

    if PHASES > 8:
      with nc.named_scope("P7_router"):
        bs = kb.snap()
        esR = ExitStack()
        probs = sbt(esR, "r_pr", (128, E), F32)
        nmax = sbt(esR, "r_nm", (128, 4), F32)
        selm = sbt(esR, "r_sel", (128, E), F32)
        u2 = sbt(esR, "r_u2", (128, E), F32)
        ctm = sbt(esR, "r_ctm", (128, 2, E), BF16)
        crow_st = sbt(esR, "r_crst", (E, MY_T), BF16)
        psR = pst(esR, "r_psR", [128, 2, 512], F32)
        psG = pst(esR, "r_psG", [128, 2, 512], F32)
        psC = pst(esR, "r_psC", [128, 2, 1024], BF16)

        @blk.tensor
        def _(t):
            kb.wait_bar(t, TE, bar7)
            for tc in range(TCH_MY):
                p = tc & 1
                kb.wait(t, TE, "r_ex", bs["r_ex"] + tc - 1)
                for a in range(NFC):
                    mm = t.matmul(psR[:, p, 0:E],
                                  xn2f_sb[:, a * MY_T + tc * 128:
                                          a * MY_T + (tc + 1) * 128],
                                  wrb[:, a * E:(a + 1) * E],
                                  start=(a == 0), stop=(a == NFC - 1))
                kb.wait(t, TE, "r_gt", bs["r_gt"] + tc - 1)
                for a in range(NFC):
                    mm2 = t.matmul(psG[:, p, 0:1],
                                   xn2f_sb[:, a * MY_T + tc * 128:
                                           a * MY_T + (tc + 1) * 128],
                                   wsgb[:, a:a + 1],
                                   start=(a == 0), stop=(a == NFC - 1))
                kb.inc(mm2, "c_mm", 1)
                if tc > 0:
                    kb.wait(t, TE, "r_dv", bs["r_dv"] + 4 * tc)
                    kb.wait(t, TE, "a_ev", bs["a_ev"] + tc - 2)
                    kb.inc(t.transpose(psC[0:E, (tc - 1) & 1, 0:128],
                                       ctm[:, (tc - 1) & 1, :], identb[:, :]),
                           "c_ct", 1)
            kb.wait(t, TE, "r_dv", bs["r_dv"] + 4 * TCH_MY)
            kb.wait(t, TE, "a_ev", bs["a_ev"] + TCH_MY - 2)
            kb.inc(t.transpose(psC[0:E, (TCH_MY - 1) & 1, 0:128],
                               ctm[:, (TCH_MY - 1) & 1, :], identb[:, :]),
                   "c_ct", 1)

        @blk.vector
        def _(v):
            kb.wait_bar(v, VE, bar7)
            for tc in range(TCH_MY):
                p = tc & 1
                kb.wait(v, VE, "c_mm", bs["c_mm"] + tc + 1)
                kb.wait(v, VE, "r_ex", bs["r_ex"] + tc)
                kb.inc(v.tensor_reduce(out=nmax[:, 0:1], in_=psR[:, p, 0:E],
                                       axis=AX.X, op=ALU.max, negate=True),
                       "r_dv", 1)
                kb.wait(v, VE, "r_ex", bs["r_ex"] + tc + 1)
                kb.inc(v.reciprocal(out=nmax[:, 1:2], in_=nmax[:, 1:2]), "r_dv", 1)
                kb.inc(v.tensor_reduce(out=nmax[:, 2:3], in_=probs[:, :],
                                       axis=AX.X, op=ALU.max), "r_dv", 1)
                kb.wait(v, VE, "r_dv", bs["r_dv"] + 4 * tc + 3)
                kb.wait(v, VE, "vch",
                        kb.inc(v.tensor_scalar(out=selm[:, :], in0=probs[:, :],
                                               scalar1=nmax[:, 2:3], scalar2=None,
                                               op0=ALU.is_lt), "vch", 1))
                kb.wait(v, VE, "vch",
                        kb.inc(v.tensor_tensor(out=u2[:, :], in0=probs[:, :],
                                               in1=selm[:, :], op=ALU.mult),
                               "vch", 1))
                kb.inc(v.tensor_reduce(out=nmax[:, 3:4], in_=u2[:, :],
                                       axis=AX.X, op=ALU.max), "misc", 1)
                kb.wait(v, VE, "misc", kb.n["misc"])
                kb.wait(v, VE, "vch",
                        kb.inc(v.tensor_scalar(out=selm[:, :], in0=probs[:, :],
                                               scalar1=nmax[:, 3:4], scalar2=None,
                                               op0=ALU.is_ge), "vch", 1))
                kb.wait(v, VE, "vch",
                        kb.inc(v.scalar_tensor_tensor(out=u2[:, :], in0=probs[:, :],
                                                      scalar=nmax[:, 1:2],
                                                      in1=selm[:, :],
                                                      op0=ALU.mult, op1=ALU.mult),
                               "vch", 1))
                kb.wait(v, VE, "c_ct", bs["c_ct"] + tc - 1)
                kb.inc(v.tensor_copy(ctm[:, p, :], u2[:, :]), "r_dv", 1)
                kb.inc(v.tensor_copy(gates[:, tc:tc + 1], psG[:, p, 0:1]), "r_gt", 1)

        @blk.scalar
        def _(s):
            kb.wait_bar(s, SC, bar7)
            for tc in range(TCH_MY):
                p = tc & 1
                kb.wait(s, SC, "r_dv", bs["r_dv"] + 4 * tc + 1)
                kb.inc(s.activation(out=probs[:, :], in_=psR[:, p, 0:E],
                                    func=AF.Exp, bias=nmax[:, 0:1], scale=1.0,
                                    accum_out=nmax[:, 1:2]), "r_ex", 1)
                if tc > 0:
                    kb.wait(s, SC, "c_ct", bs["c_ct"] + tc)
                    kb.inc(s.copy(crow_st[0:E, (tc - 1) * 128:tc * 128],
                                  psC[0:E, (tc - 1) & 1, 0:128]), "a_ev", 1)
            kb.wait(s, SC, "c_ct", bs["c_ct"] + TCH_MY)
            kb.inc(s.copy(crow_st[0:E, (TCH_MY - 1) * 128:TCH_MY * 128],
                          psC[0:E, (TCH_MY - 1) & 1, 0:128]), "a_ev", 1)
            kb.wait(s, SC, "r_gt", bs["r_gt"] + TCH_MY)
            kb.inc(s.activation(out=gates[:, :], in_=gates[:, :], func=AF.Sigmoid),
                   "misc", 1)

        @blk.gpsimd
        def _(g):
            kb.wait_bar(g, GP, bar7)
            kb.wait(g, GP, "a_ev", bs["a_ev"] + TCH_MY)
            kb.inc(g.dma_start(
                out=s_cr.rearrange("e hh w -> e (hh w)"),
                in_=crow_st[0:E, :]), "st", 16)

        esR.close()
    esC2.close()
    bar8 = kb.snap()

    if PHASES > 9:
      with nc.named_scope("P8_ffn"):
        bs = kb.snap()
        esF = ExitStack()
        acc = sbt(esF, "f_acc", (128, 8 * D), F32)
        xnh = sbt(esF, "f_xnh", (128, NFC * 1024), BF16)
        wstg = sbt(esF, "f_wstg", (128, 2, D), F32)
        wgb = sbt(esF, "f_wgb", (128, 2, D), BF16)
        wub = sbt(esF, "f_wub", (128, 2, D), BF16)
        wdb = sbt(esF, "f_wdb", (128, MCH * D), BF16)
        mid = sbt(esF, "f_mid", (128, MCH * 1024), BF16)
        gsil = sbt(esF, "f_gsil", (128, 2, 512), BF16)
        cbt = sbt(esF, "f_cb", (128, 2, 512), BF16)
        crsl = sbt(esF, "f_crsl", (1, 2, 1024), BF16)
        pg = pst(esF, "f_pg", [128, 2, 512], F32)
        pu = pst(esF, "f_pu", [128, 2, 512], F32)
        pd = pst(esF, "f_pd", [128, 2, 512], F32)
        pcb = pst(esF, "f_pcb", [128, 2, 512], F32)
        wdv = wd.rearrange("e (m p) d -> e m p d", p=128)
        sdv = sd.rearrange("(m p) d -> m p d", p=128)
        outv = out.rearrange("(c p) d -> c p d", p=128)

        jobs = [("e", e) for e in range(E)] + [("s", g2) for g2 in range(4)]
        NJ = len(jobs)
        JW = 3 * MCH
        GU = 2 * MCH

        def wsrcs(kind, idx, m):
            if kind == "e":
                return (wgr[idx, m], wur[idx, m], wdv[idx, m])
            mg = idx * MCH + m
            return (sgr[mg], sur[mg], sdv[mg])

        for half in range(2):
            bsH = kb.snap()
            crsl_tok = {}
            wg_tok = {}
            wu_tok = {}
            wd_tok = {}
            base_tok = [0]

            @blk.sync
            def _(sy):
                if half == 0:
                    kb.wait_bar(sy, SY, bar8)
                else:
                    kb.wait(sy, SY, "d_u", bsH["d_u"])
                for i in range(NFC):
                    kb.inc(sy.dma_start(out=xnh[:, i * 1024:(i + 1) * 1024],
                                        in_=s_xn2[i][:, half * 1024:(half + 1) * 1024]),
                           "lda", 16)
                for j, (kind, idx) in enumerate(jobs):
                    if j == 1:
                        if half == 1:
                            kb.wait(sy, SY, "st", bsH["st"])
                        for tch in range(8):
                            kb.inc(sy.dma_start(out=acc[:, tch * D:(tch + 1) * D],
                                                in_=sx2v_g[half * 8 + tch]),
                                   "lda", 16)
                        base_tok[0] = kb.n["lda"]
                    if kind == "e":
                        kb.wait(sy, SY, "d_cbm", bsH["d_cbm"] + 2 * idx - 2)
                        sem = "lc%d" % (idx & 1)
                        kb.inc(sy.dma_start(out=crsl[0:1, idx & 1, :],
                                            in_=s_cr[idx, half:half + 1]), sem, 16)
                        crsl_tok[idx] = (sem, kb.n[sem])
                    for m in range(MCH):
                        jm = j * MCH + m
                        it = 3 * jm
                        gg, uu, dd = wsrcs(kind, idx, m)
                        kb.wait(sy, SY, "d_wd", bsH["d_wd"] + it - 1)
                        sem = "lw%d" % (it & 1)
                        kb.inc(sy.dma_start(out=wstg[:, it & 1, :], in_=gg),
                               sem, 16)
                        wg_tok[jm] = (sem, kb.n[sem])
                        kb.wait(sy, SY, "d_wd", bsH["d_wd"] + it)
                        sem = "lw%d" % ((it + 1) & 1)
                        kb.inc(sy.dma_start(out=wstg[:, (it + 1) & 1, :], in_=uu),
                               sem, 16)
                        wu_tok[jm] = (sem, kb.n[sem])
                        kb.wait(sy, SY, "d_wd", bsH["d_wd"] + it + 1)
                        sem = "lw%d" % ((it + 2) & 1)
                        kb.inc(sy.dma_start(out=wstg[:, (it + 2) & 1, :], in_=dd),
                               sem, 16)
                        wd_tok[jm] = (sem, kb.n[sem])

            @blk.vector
            def _(v):
                if half == 0:
                    kb.wait_bar(v, VE, bar8)
                for j, (kind, idx) in enumerate(jobs):
                    if kind == "e":
                        for tt in range(2):
                            cbi = 2 * idx + tt
                            kb.wait(v, VE, "d_cbm", bsH["d_cbm"] + cbi + 1)
                            kb.wait(v, VE, "d_mid", bsH["d_mid"] + (j - 1) * GU)
                            kb.inc(v.tensor_copy(cbt[:, tt, :], pcb[:, tt, :]),
                                   "d_cb", 1)
                    def emit_mid(m, tt):
                        gt = j * GU + 2 * m + tt
                        kb.wait(v, VE, "d_u", bsH["d_u"] + gt + 1)
                        kb.wait(v, VE, "d_sil", bsH["d_sil"] + gt + 1)
                        kb.wait(v, VE, "d_dn", bsH["d_dn"] + 32 * (j - 1))
                        mslc = mid[:, m * 1024 + tt * 512:m * 1024 + (tt + 1) * 512]
                        if kind == "e":
                            kb.wait(v, VE, "vch",
                                    kb.inc(v.tensor_tensor(
                                        out=mslc, in0=pu[:, gt & 1, :],
                                        in1=gsil[:, gt & 1, :], op=ALU.mult),
                                        "vch", 1))
                            kb.inc(v.tensor_tensor(
                                out=mslc, in0=mslc, in1=cbt[:, tt, :],
                                op=ALU.mult), "d_mid", 1)
                        else:
                            kb.inc(v.tensor_tensor(
                                out=mslc, in0=pu[:, gt & 1, :],
                                in1=gsil[:, gt & 1, :], op=ALU.mult), "d_mid", 1)

                    for m in range(MCH):
                        jm = j * MCH + m
                        it = 3 * jm
                        gt0 = j * GU + 2 * m
                        slot_free = gt0 if m == 0 else gt0 - 2
                        kb.wait(v, VE, wg_tok[jm][0], wg_tok[jm][1])
                        kb.wait(v, VE, "d_g", bsH["d_g"] + slot_free)
                        kb.inc(v.tensor_copy(wgb[:, m & 1, :], wstg[:, it & 1, :]),
                               "d_wd", 1)
                        kb.wait(v, VE, wu_tok[jm][0], wu_tok[jm][1])
                        kb.wait(v, VE, "d_u", bsH["d_u"] + slot_free)
                        kb.inc(v.tensor_copy(wub[:, m & 1, :], wstg[:, (it + 1) & 1, :]),
                               "d_wd", 1)
                        kb.wait(v, VE, wd_tok[jm][0], wd_tok[jm][1])
                        kb.wait(v, VE, "d_dn", bsH["d_dn"] + 32 * j)
                        kb.inc(v.tensor_copy(wdb[:, m * D:(m + 1) * D],
                                             wstg[:, (it + 2) & 1, :]), "d_wd", 1)
                        if m >= 1:
                            emit_mid(m - 1, 0)
                            emit_mid(m - 1, 1)
                    emit_mid(MCH - 1, 0)
                    emit_mid(MCH - 1, 1)
                    kb.wait(v, VE, "lda", base_tok[0])
                    for di in range(32):
                        tt, tsub, fo = di // 16, (di // 4) % 4, di % 4
                        tch = tt * 4 + tsub
                        dgi = j * 32 + di
                        kb.wait(v, VE, "d_dn", bsH["d_dn"] + dgi + 1)
                        kb.wait(v, VE, "d_acc", bsH["d_acc"] + (j - 1) * 32 + di + 1)
                        aslc = acc[:, tch * D + fo * 512:tch * D + (fo + 1) * 512]
                        if kind == "e":
                            kb.inc(v.tensor_tensor(out=aslc, in0=aslc,
                                                   in1=pd[:, dgi & 1, :], op=ALU.add),
                                   "d_acc", 1)
                        else:
                            kb.inc(v.scalar_tensor_tensor(
                                out=aslc, in0=pd[:, dgi & 1, :],
                                scalar=gates[:, half * 8 + tch:half * 8 + tch + 1],
                                in1=aslc, op0=ALU.mult, op1=ALU.add), "d_acc", 1)

            @blk.scalar
            def _(s):
                if half == 0:
                    kb.wait_bar(s, SC, bar8)
                for j, (kind, idx) in enumerate(jobs):
                    for m in range(MCH):
                        for tt in range(2):
                            gt = j * GU + 2 * m + tt
                            kb.wait(s, SC, "d_g", bsH["d_g"] + gt + 1)
                            kb.wait(s, SC, "d_mid", bsH["d_mid"] + gt - 1)
                            kb.inc(s.activation(out=gsil[:, gt & 1, :],
                                                in_=pg[:, gt & 1, :], func=AF.Silu),
                                   "d_sil", 1)

            @blk.tensor
            def _(t):
                if half == 0:
                    kb.wait_bar(t, TE, bar8)
                for j, (kind, idx) in enumerate(jobs):
                    if kind == "e":
                        kb.wait(t, TE, crsl_tok[idx][0], crsl_tok[idx][1])
                        for tt in range(2):
                            cbi = 2 * idx + tt
                            kb.wait(t, TE, "d_cb", bsH["d_cb"] + cbi - 1)
                            kb.inc(t.matmul(pcb[:, tt, :], ones1[:, :],
                                            crsl[0:1, idx & 1,
                                                 tt * 512:(tt + 1) * 512],
                                            start=True, stop=True), "d_cbm", 1)
                    for m in range(MCH):
                        kb.wait(t, TE, "d_wd", bsH["d_wd"] + 3 * (j * MCH + m) + 1)
                        for tt in range(2):
                            gt = j * GU + 2 * m + tt
                            kb.wait(t, TE, "d_sil", bsH["d_sil"] + gt - 1)
                            for a in range(NFC):
                                kb.wait(t, TE, "lda", bsH["lda"] + 16 * (a + 1))
                                mm = t.matmul(
                                    pg[:, gt & 1, :],
                                    wgb[:, m & 1, a * 128:(a + 1) * 128],
                                    xnh[:, a * 1024 + tt * 512:a * 1024 + (tt + 1) * 512],
                                    start=(a == 0), stop=(a == NFC - 1))
                            kb.inc(mm, "d_g", 1)
                            kb.wait(t, TE, "d_wd",
                                    bsH["d_wd"] + 3 * (j * MCH + m) + 2)
                            kb.wait(t, TE, "d_mid", bsH["d_mid"] + gt - 1)
                            for a in range(NFC):
                                mm = t.matmul(
                                    pu[:, gt & 1, :],
                                    wub[:, m & 1, a * 128:(a + 1) * 128],
                                    xnh[:, a * 1024 + tt * 512:a * 1024 + (tt + 1) * 512],
                                    start=(a == 0), stop=(a == NFC - 1))
                            kb.inc(mm, "d_u", 1)
                    kb.wait(t, TE, "d_mid", bsH["d_mid"] + (j + 1) * GU)
                    kb.wait(t, TE, "d_wd", bsH["d_wd"] + 3 * (j + 1) * MCH)
                    for di in range(32):
                        tt, tsub, fo = di // 16, (di // 4) % 4, di % 4
                        dgi = j * 32 + di
                        kb.wait(t, TE, "d_acc", bsH["d_acc"] + dgi - 1)
                        for m in range(MCH):
                            mm = t.matmul(
                                pd[:, dgi & 1, :],
                                mid[:, m * 1024 + tt * 512 + tsub * 128:
                                    m * 1024 + tt * 512 + tsub * 128 + 128],
                                wdb[:, m * D + fo * 512:m * D + (fo + 1) * 512],
                                start=(m == 0), stop=(m == MCH - 1))
                        kb.inc(mm, "d_dn", 1)

            @blk.gpsimd
            def _(g):
                if half == 0:
                    kb.wait_bar(g, GP, bar8)
                for tch in range(8):
                    di_last = (tch // 4) * 16 + (tch % 4) * 4 + 3
                    kb.wait(g, GP, "d_acc",
                            bsH["d_acc"] + (NJ - 1) * 32 + di_last + 1)
                    kb.inc(g.dma_start(out=outv[half * 8 + tch],
                                       in_=acc[:, tch * D:(tch + 1) * D]), "st", 16)

        esF.close()

    @blk.gpsimd
    def _(g):
        kb.wait(g, GP, "st", kb.n["st"])

    es.close()
    return nc


_NC_CACHE = {}


def _get_nc():
    key = (PHASES, DBG_SCRATCH)
    if key not in _NC_CACHE:
        _NC_CACHE[key] = build_program()
    return _NC_CACHE[key]


def make_in_maps(inputs):
    f32 = lambda a: np.ascontiguousarray(np.asarray(a, dtype=np.float32))
    x = f32(inputs["L0_attn_x"]).reshape(B, S, D)
    wq = f32(inputs["L0_attn_Wq"])
    wk = f32(inputs["L0_attn_Wk"])
    wv = f32(inputs["L0_attn_Wv"])
    wo = f32(inputs["L0_attn_Wo"])
    wg = f32(inputs["L0_ffn_Wg"])
    wu = f32(inputs["L0_ffn_Wu"])
    sg = f32(inputs["L0_ffn_Sg"])
    su = f32(inputs["L0_ffn_Su"])

    def headcols(w):
        return np.ascontiguousarray(
            w.reshape(NH, 128, NFC, 128).transpose(0, 3, 2, 1).reshape(NH, 128, D))

    def vgroups(w):
        return np.ascontiguousarray(
            w.reshape(4, 512, NFC, 128).transpose(0, 3, 2, 1).reshape(4, 128, NFC * 512))

    def mcols(w):
        return np.ascontiguousarray(
            w.reshape(E, NFC, 128, MCH, 128).transpose(0, 3, 2, 1, 4).reshape(E, MCH, 128, D))

    def scols(w):
        return np.ascontiguousarray(
            w.reshape(NFC, 128, 4 * MCH, 128).transpose(2, 1, 0, 3).reshape(4 * MCH, 128, D))

    base = {
        "wqTr": headcols(wq),
        "wkTr": headcols(wk),
        "wvTr": vgroups(wv),
        "woT": np.ascontiguousarray(wo.T),
        "bq": f32(inputs["L0_attn_bq"]),
        "bk": f32(inputs["L0_attn_bk"]),
        "bv_row": f32(inputs["L0_attn_bv"]).reshape(1, D),
        "bo_row": f32(inputs["L0_attn_bo"]).reshape(1, D),
        "wr": f32(inputs["L0_ffn_Wr"]),
        "wsg": f32(inputs["L0_ffn_Wsg"]),
        "wgr": mcols(wg),
        "wur": mcols(wu),
        "wd": f32(inputs["L0_ffn_Wd"]),
        "sgr": scols(sg),
        "sur": scols(su),
        "sd": f32(inputs["L0_ffn_Sd"]),
        "permT": _perm_T().astype(_BF),
        "ident": np.eye(128, dtype=np.float32).astype(_BF),
    }
    cosf, sinf = _rope_tables()
    rows = {h: _my_rows(h) for h in (0, 1)}
    cmasks = {h: _cmask(h) for h in (0, 1)}

    in_maps = []
    for c in range(8):
        b, half = c // 2, c % 2
        m = dict(base)
        m["x_my"] = np.ascontiguousarray(x[b][rows[half]])
        m["cosq"] = np.ascontiguousarray(cosf[:, rows[half]]).astype(_BF)
        m["sinq"] = np.ascontiguousarray(sinf[:, rows[half]]).astype(_BF)
        m["cmask"] = cmasks[half].astype(_BF)
        in_maps.append(m)
    return in_maps, rows


def kernel(**inputs):
    in_maps, rows = make_in_maps(inputs)
    nc = _get_nc()
    res = run_bass_kernel_spmd(nc, in_maps, core_ids=list(range(8)))
    outp = np.empty((B, S, D), dtype=np.float32)
    for c in range(8):
        b, half = c // 2, c % 2
        outp[b][rows[half]] = np.asarray(res.results[c]["out"], dtype=np.float32)
    return outp
